# version 11
# baseline (speedup 1.0000x reference)
import contextlib
import numpy as np
import concourse.bass as bass
import concourse.mybir as mybir
from concourse.bass import IndirectOffsetOnAxis
from concourse.bass_utils import run_bass_kernel_spmd

F32 = mybir.dt.float32
BF16 = mybir.dt.bfloat16
I32 = mybir.dt.int32
AF = mybir.ActivationFunctionType
OP = mybir.AluOpType

NCORES = 8
B_FULL, T, C = 16, 2048, 1024
NB = B_FULL // NCORES
NTOK = NB * T
CC = C // 128
I_FF = 4096
IC = I_FF // 128
NEXP, TOPK = 8, 2
CAPACITY = int(1.25 * B_FULL * T * TOPK / NEXP) + 1
SLOTCAP = 1536
NSLOT = NEXP * SLOTCAP
W1T = 256
N1T = NTOK // W1T
W3T = 128
N3T = NTOK // W3T
EXP_CLIP_LO = float(np.exp(np.float32(-20.0), dtype=np.float32))
EXP_CLIP_HI = float(np.exp(np.float32(20.0), dtype=np.float32))

V_TD, V_U, V_TMK, V_TMV, V_TMR, V_TMK2, V_TMR2, V_LN1, V_LN2 = range(9)
NVEC = 9
D_EW, D_1MK, D_1MV, D_1MR, D_1MK2, D_1MR2 = range(6)
NDV = 6

DEBUG_TAPS = ()
USE_COLLECTIVE = False

ENGN = ("sync", "pool", "pe", "act", "dve")


class Step:
    __slots__ = ("eng", "reads", "writes", "emit", "q", "tag", "needs_inc", "sid", "deps")

    def __init__(self, eng, reads, writes, emit, q=None, tag=""):
        self.eng = eng
        self.reads = reads
        self.writes = writes
        self.emit = emit
        self.q = q
        self.tag = tag
        self.needs_inc = False
        self.sid = -1
        self.deps = []


class Builder:

    def __init__(self, nc):
        self.nc = nc
        self.steps = []
        self.last_w = {}
        self.readers = {}
        self.barrier_count = 0
        self.ctx = contextlib.ExitStack()

    def step(self, eng, reads, writes, emit, q=None, tag=""):
        s = Step(eng, list(reads), list(writes), emit, q, tag)
        s.sid = len(self.steps)
        self.steps.append(s)
        for k in s.reads:
            self.readers.setdefault(k, []).append(s)
        for k in s.writes:
            self.last_w[k] = s
            self.readers[k] = []
        return s

    def barrier(self):
        s = Step("BARRIER", [], [], None, None, f"bar{self.barrier_count}")
        s.sid = len(self.steps)
        self.steps.append(s)
        self.barrier_count += 1
        self.last_w = {}
        self.readers = {}

    def _resolve(self):
        last_w, readers = {}, {}
        for s in self.steps:
            if s.eng == "BARRIER":
                last_w, readers = {}, {}
                s.deps = []
                continue
            deps = []
            for k in s.reads:
                w = last_w.get(k)
                if w is not None:
                    deps.append(w)
            for k in s.writes:
                for r in readers.get(k, ()):
                    deps.append(r)
                w = last_w.get(k)
                if w is not None:
                    deps.append(w)
            uniq = {}
            for d in deps:
                if d is s:
                    continue
                if d.q is None and d.eng == s.eng:
                    continue
                uniq[d.sid] = d
            s.deps = [uniq[i] for i in sorted(uniq)]
            for d in s.deps:
                d.needs_inc = True
            for k in s.reads:
                readers.setdefault(k, []).append(s)
            for k in s.writes:
                last_w[k] = s
                readers[k] = []
            if s.tag == "scan":
                for k in s.reads:
                    w = last_w.get(k)
                    if w is not None and w.q is None and w.eng == "dve" and w is not s:
                        covered = any(w in d.deps for d in s.deps)
                        if not covered:
                            raise AssertionError(
                                f"scan step reads {k} produced by same-engine dve "
                                f"step {w.tag} without a cross-engine relay")

    def lower(self, block, engines, sems, dma_sems, bar):
        self._resolve()
        eng_cnt = {e: 0 for e in ENGN}
        q_cnt = {q: 0 for q in dma_sems}
        step_val = {}
        for s in self.steps:
            if s.eng == "BARRIER":
                continue
            if s.q is not None:
                q_cnt[s.q] += 16
                step_val[s.sid] = ("q", s.q, q_cnt[s.q])
            elif s.needs_inc:
                eng_cnt[s.eng] += 1
                step_val[s.sid] = ("e", s.eng, eng_cnt[s.eng])

        highwater = {e: {} for e in ENGN}
        cur_q = {q: 0 for q in dma_sems}

        per_eng = {e: [] for e in ENGN}
        nbar = 0
        q_running = {q: 0 for q in dma_sems}
        e_running = {e: 0 for e in ENGN}
        for s in self.steps:
            if s.eng == "BARRIER":
                nbar += 1
                snap_q = dict(q_running)
                snap_e = dict(e_running)
                for e in ENGN:
                    per_eng[e].append(("bar", nbar, snap_q, snap_e))
                continue
            per_eng[s.eng].append(("step", s))
            if s.q is not None:
                q_running[s.q] += 16
            elif s.needs_inc:
                e_running[s.eng] += 1

        for e in ENGN:
            def _section(eng, e=e):
                hw = highwater[e]
                for item in per_eng[e]:
                    if item[0] == "bar":
                        _, k, snap_q, snap_e = item
                        for q, v in snap_q.items():
                            if v and hw.get(("q", q), 0) < v:
                                eng.wait_ge(dma_sems[q], v)
                                hw[("q", q)] = v
                        eng.nop().then_inc(bar, 1)
                        eng.wait_ge(bar, len(ENGN) * k)
                        for e2, v in snap_e.items():
                            if v:
                                hw[("e", e2)] = max(hw.get(("e", e2), 0), v)
                        for q, v in snap_q.items():
                            if v:
                                hw[("q", q)] = max(hw.get(("q", q), 0), v)
                        continue
                    st = item[1]
                    for d in st.deps:
                        sv = step_val.get(d.sid)
                        if sv is None:
                            continue
                        kind, name, val = sv
                        key = (kind, name)
                        if hw.get(key, 0) >= val:
                            continue
                        if kind == "e":
                            eng.wait_ge(sems[name], val)
                        else:
                            eng.wait_ge(dma_sems[name], val)
                        hw[key] = val
                    inst = st.emit(eng)
                    if st.q is not None:
                        inst.then_inc(dma_sems[st.q], 16)
                    elif st.needs_inc:
                        if inst is None:
                            raise AssertionError(f"step {st.tag} emitted nothing")
                        inst.then_inc(sems[e], 1)
            getattr(block, self._section_name(e))(_section)

    @staticmethod
    def _section_name(e):
        return {"sync": "sync", "pool": "gpsimd", "pe": "tensor",
                "act": "scalar", "dve": "vector"}[e]



def build_program():
    nc = bass.Bass("TRN2", target_bir_lowering=False)
    b = Builder(nc)
    ctx = b.ctx

    def din(name, shape, dt=F32):
        return nc.dram_tensor(name, list(shape), dt, kind="ExternalInput").ap()

    x0f = din("x0f", [C, NTOK])
    wkey = din("wkey", [C, C]); wval = din("wval", [C, C]); wrec = din("wrec", [C, C])
    wout = din("wout", [C, C]); wrec2 = din("wrec2", [C, C])
    wkey2 = din("wkey2", [C, I_FF]); wval2 = din("wval2", [I_FF, C])
    rwt = din("rwt", [C, NEXP])
    w1 = din("w1", [NEXP, C, I_FF]); w2 = din("w2", [NEXP, I_FF, C])
    vecs_in = din("vecs_i", [128, CC, NVEC])
    consts_in = din("consts_i", [128, 528])
    identbf_in = din("identbf_i", [128, 128], BF16)
    ltcore_in = din("ltcore", [NEXP, 1])

    out_d = nc.dram_tensor("outp", [NTOK, C], F32, kind="ExternalOutput").ap()

    def scratch(name, shape, dt=F32):
        kind = "ExternalOutput" if name in DEBUG_TAPS else "Internal"
        return nc.dram_tensor(name, list(shape), dt, kind=kind).ap()

    ks_d = scratch("ks", [128, CC, NTOK])
    vs_d = scratch("vs", [128, CC, NTOK])
    rs_d = scratch("rs", [128, CC, NTOK])
    wo_d = scratch("wo_s", [128, CC, NTOK])
    xf_d = scratch("xf_s", [NTOK, C], BF16)
    s_d = scratch("s_s", [NTOK, C])
    buf_d = scratch("buf_s", [NSLOT, C], BF16)
    po_d = scratch("po_s", [NSLOT, C])
    ccin_d = scratch("cc_in", [1, NEXP])
    if "dbg" in DEBUG_TAPS:
        dbg_xa = nc.dram_tensor("dbg_xa", [128, CC, NTOK], F32, kind="ExternalOutput").ap()
        dbg_r2 = nc.dram_tensor("dbg_r2", [128, CC, NTOK], F32, kind="ExternalOutput").ap()
        dbg_cm = nc.dram_tensor("dbg_cm", [128, CC, NTOK], F32, kind="ExternalOutput").ap()
        dbg_x1 = nc.dram_tensor("dbg_x1", [128, CC, NTOK], F32, kind="ExternalOutput").ap()
        dbg_k2 = nc.dram_tensor("dbg_k2", [128, IC, NTOK], F32, kind="ExternalOutput").ap()
        dbg_sl1 = nc.dram_tensor("dbg_sl1", [128, N3T], I32, kind="ExternalOutput").ap()
        dbg_sl2 = nc.dram_tensor("dbg_sl2", [128, N3T], I32, kind="ExternalOutput").ap()
        dbg_g1 = nc.dram_tensor("dbg_g1", [128, N3T], F32, kind="ExternalOutput").ap()
        dbg_g2 = nc.dram_tensor("dbg_g2", [128, N3T], F32, kind="ExternalOutput").ap()
        dbg_lt = nc.dram_tensor("dbg_lt", [128, N3T, NEXP], F32, kind="ExternalOutput").ap()
        dbg_e1 = nc.dram_tensor("dbg_e1", [128, N3T], F32, kind="ExternalOutput").ap()
        dbg_r1 = nc.dram_tensor("dbg_r1", [128, N3T], F32, kind="ExternalOutput").ap()
    ccout_d = nc.dram_tensor("cc_out", [NEXP, NEXP], F32, kind="Internal",
                             addr_space="Shared").ap()

    def sb(name, shape, dt=F32):
        return ctx.enter_context(nc.sbuf_tensor(name, list(shape), dt, side="right"))

    vec = sb("vec", [128, CC, NVEC])
    dvec = sb("dvec", [128, CC, 11])
    consts = sb("consts", [128, 528])
    identbf = sb("identbf", [128, 128], BF16)
    ltsb = sb("ltsb", [NEXP, 1])
    routersb = sb("routersb", [128, CC, NEXP], BF16)
    lt_all = sb("lt_all", [128, N3T, NEXP])
    e1a = sb("e1a", [128, N3T]); e2a = sb("e2a", [128, N3T])
    r1a = sb("r1a", [128, N3T]); r2a = sb("r2a", [128, N3T])
    g1a = sb("g1a", [128, N3T]); g2a = sb("g2a", [128, N3T])
    gts1 = sb("gts1", [128, N3T]); gts2 = sb("gts2", [128, N3T])
    sl1i = sb("sl1i", [128, N3T], I32); sl2i = sb("sl2i", [128, N3T], I32)

    ident = consts[:, 0:128]
    LT128 = consts[:, 128:256]
    iota8 = consts[:, 256:264]
    cstc = consts[:, 264:272]
    C_ONE, C_M1, C_MH = cstc[:, 0:1], cstc[:, 1:2], cstc[:, 2:3]
    C_E11, C_E5, C_E6, C_E8, C_1P = (cstc[:, 3:4], cstc[:, 4:5], cstc[:, 5:6],
                                     cstc[:, 6:7], cstc[:, 7:8])
    onesC = consts[:, 272:400]
    ones128 = consts[:, 400:528]

    DX_EW = 0
    DX_KA, DX_KB, DX_VA, DX_VB, DX_RA, DX_RB = 1, 2, 3, 4, 5, 6
    DX_K2A, DX_K2B, DX_R2A, DX_R2B = 7, 8, 9, 10

    def bc(ap2d, reps):
        return ap2d.unsqueeze(1).broadcast_to([128, reps, ap2d.shape[-1]])

    p02 = ctx.enter_context(contextlib.ExitStack())
    wo_sb = p02.enter_context(nc.sbuf_tensor("wo_sb", [128, CC, C], BF16))
    p01 = contextlib.ExitStack()
    zt = p01.enter_context(nc.sbuf_tensor("zt", [128, 4096], BF16))
    tmpv = p01.enter_context(nc.sbuf_tensor("tmpv", [128, CC], F32))

    b.step("sync", [], ["vec"], lambda e: e.dma_start(out=vec[:, :, :], in_=vecs_in[:, :, :]), q="dsy", tag="ld_vec")
    b.step("sync", [], ["consts"], lambda e: e.dma_start(out=consts[:, :], in_=consts_in[:, :]), q="dsy", tag="ld_consts")
    b.step("sync", [], ["identbf"], lambda e: e.dma_start(out=identbf[:, :], in_=identbf_in[:, :]), q="dsy", tag="ld_identbf")
    b.step("sync", [], ["ltsb"], lambda e: e.dma_start(out=ltsb[:, :], in_=ltcore_in[:, :]), q="dsy", tag="ld_ltsb")
    b.step("pool", [], ["routersb"], lambda e: e.dma_start(out=routersb[:, :, :], in_=rwt.rearrange("(cc p) e -> p cc e", p=128)), q="dgp", tag="ld_router")

    b.step("act", ["vec"], ["tmpv"], lambda e: e.activation(out=tmpv[:, :], in_=vec[:, :, V_TD], func=AF.Exp), tag="ew1")
    b.step("pool", ["tmpv"], ["tmpv"], lambda e: e.tensor_scalar(tmpv[:, :], tmpv[:, :], 20.0, None, OP.min), tag="ew2")
    b.step("act", ["tmpv", "consts"], ["dvec"], lambda e: e.activation(out=dvec[:, :, DX_EW], in_=tmpv[:, :], func=AF.Exp, scale=C_M1), tag="ew3")

    def mk_coeff(dst_a, dst_b, v_tm, v_ln, tagn):
        def _a(e, dst_a=dst_a, v_tm=v_tm, v_ln=v_ln):
            return e.tensor_tensor(dvec[:, :, dst_a], vec[:, :, v_tm], vec[:, :, v_ln], OP.mult)
        def _b(e, dst_a=dst_a, dst_b=dst_b, v_ln=v_ln):
            return e.tensor_tensor(dvec[:, :, dst_b], vec[:, :, v_ln], dvec[:, :, dst_a], OP.subtract)
        b.step("pool", ["vec"], ["dvec"], _a, tag=f"cf{tagn}a")
        b.step("pool", ["vec", "dvec"], ["dvec"], _b, tag=f"cf{tagn}b")

    mk_coeff(DX_KA, DX_KB, V_TMK, V_LN1, "k")
    mk_coeff(DX_VA, DX_VB, V_TMV, V_LN1, "v")
    mk_coeff(DX_RA, DX_RB, V_TMR, V_LN1, "r")
    mk_coeff(DX_K2A, DX_K2B, V_TMK2, V_LN2, "k2")
    mk_coeff(DX_R2A, DX_R2B, V_TMR2, V_LN2, "r2")

    b.step("pool", [], ["zt"], lambda e: e.memset(zt[:, :], 0.0), tag="zt0")
    for r0 in range(0, NSLOT, 512):
        def _z(e, r0=r0):
            return e.dma_start(
                out=buf_d[r0:r0 + 512, :].rearrange("(a b) c -> b a c", b=128),
                in_=zt[:, :].rearrange("p (a c) -> p a c", a=4))
        b.step("act", ["zt"], ["bufs"], _z, q="dsc", tag=f"zb{r0}")

    wk_sb = p01.enter_context(nc.sbuf_tensor("wk_sb", [128, CC, C], BF16))
    wv_sb = p01.enter_context(nc.sbuf_tensor("wv_sb", [128, CC, C], BF16))
    wr_sb = p01.enter_context(nc.sbuf_tensor("wr_sb", [128, CC, C], BF16))
    for nm, dst, src in (("k", wk_sb, wkey), ("v", wv_sb, wval),
                         ("r", wr_sb, wrec), ("wo", wo_sb, wout)):
        def _w(e, dst=dst, src=src):
            return e.dma_start(out=dst[:, :, :], in_=src.rearrange("(kc p) m -> p kc m", p=128))
        b.step("pool", [], [nm], _w, q="dgp", tag=f"ld{nm}")


    x0b = [p01.enter_context(nc.sbuf_tensor(f"x0b{j}", [128, CC, W1T], F32)) for j in range(2)]
    zb = p01.enter_context(nc.sbuf_tensor("zb", [128, CC, W1T], F32))
    sqb = p01.enter_context(nc.sbuf_tensor("sqb", [128, CC, W1T], F32))
    hb = [p01.enter_context(nc.sbuf_tensor(f"hb{j}", [128, CC, W1T + 1], F32)) for j in range(2)]
    shmb = p01.enter_context(nc.sbuf_tensor("shmb", [128, CC, W1T], F32))
    xkb = [p01.enter_context(nc.sbuf_tensor(f"xkb{j}", [128, CC, W1T], BF16)) for j in range(2)]
    xvb = [p01.enter_context(nc.sbuf_tensor(f"xvb{j}", [128, CC, W1T], BF16)) for j in range(2)]
    xrb = [p01.enter_context(nc.sbuf_tensor(f"xrb{j}", [128, CC, W1T], BF16)) for j in range(2)]
    kst = [p01.enter_context(nc.sbuf_tensor(f"kst{j}", [128, CC, W1T], F32)) for j in range(2)]
    vst = [p01.enter_context(nc.sbuf_tensor(f"vst{j}", [128, CC, W1T], F32)) for j in range(2)]
    rst = [p01.enter_context(nc.sbuf_tensor(f"rst{j}", [128, CC, W1T], F32)) for j in range(2)]
    bbca = p01.enter_context(nc.sbuf_tensor("bbca", [128, W1T], F32))
    abca = p01.enter_context(nc.sbuf_tensor("abca", [128, W1T], F32))
    pmb = [p01.enter_context(nc.psum_tensor(f"pmb{j}", [128, 512], F32)) for j in range(2)]
    pvb = [p01.enter_context(nc.psum_tensor(f"pvb{j}", [128, 512], F32)) for j in range(2)]
    ppj = [p01.enter_context(nc.psum_tensor(f"ppj{j}", [128, 512], F32)) for j in range(4)]

    TPB1 = T // W1T

    for i in range(N1T):
        t0 = i * W1T
        p = i % 2

        b.step("sync", [], [f"x0.{p}"],
               lambda e, t0=t0, p=p: e.dma_start(
                   out=x0b[p][:, :, :],
                   in_=x0f[:, t0:t0 + W1T].rearrange("(cc q) t -> q cc t", q=128)),
               q="dsy", tag=f"p1ld{i}")

        def _mean(e, p=p):
            r = None
            for kc in range(CC):
                r = e.matmul(pmb[p][:, :W1T], onesC, x0b[p][:, kc, :],
                             start=(kc == 0), stop=(kc == CC - 1))
            return r
        b.step("pe", [f"x0.{p}", "consts"], [f"pmb.{p}"], _mean, tag=f"p1mean{i}")

        b.step("dve", [f"x0.{p}", f"pmb.{p}"], ["zb"],
               lambda e, p=p: e.tensor_tensor(zb[:, :, :], x0b[p][:, :, :],
                                              bc(pmb[p][:, :W1T], CC), OP.subtract),
               tag=f"p1z{i}")
        b.step("act", ["zb"], ["sqb"],
               lambda e: e.activation(out=sqb[:, :, :], in_=zb[:, :, :], func=AF.Square),
               tag=f"p1sq{i}")

        def _var(e, p=p):
            r = None
            for kc in range(CC):
                r = e.matmul(pvb[p][:, :W1T], onesC, sqb[:, kc, :],
                             start=(kc == 0), stop=(kc == CC - 1))
            return r
        b.step("pe", ["sqb", "consts"], [f"pvb.{p}"], _var, tag=f"p1var{i}")

        def _beta(e, p=p):
            e.activation(out=bbca[:, :], in_=pvb[p][:, :W1T], func=AF.Ln,
                         scale=C_1P, bias=C_E11)
            return e.activation(out=bbca[:, :], in_=bbca[:, :], func=AF.Exp, scale=C_MH)
        b.step("act", [f"pvb.{p}", "consts"], ["bbca"], _beta, tag=f"p1beta{i}")

        b.step("dve", ["zb", "bbca"], [f"hb.{p}"],
               lambda e, p=p: e.tensor_tensor(hb[p][:, :, 1:W1T + 1], zb[:, :, :],
                                              bc(bbca[:, :], CC), OP.mult),
               tag=f"p1h{i}")
        if i % TPB1 == 0:
            b.step("dve", [], [f"hbc.{p}"],
                   lambda e, p=p: e.memset(hb[p][:, :, 0:1], 0.0), tag=f"p1bz{i}")
        else:
            b.step("dve", [f"hb.{1 - p}"], [f"hbc.{p}"],
                   lambda e, p=p: e.tensor_copy(hb[p][:, :, 0:1], hb[1 - p][:, :, W1T:W1T + 1]),
                   tag=f"p1bc{i}")

        for nm, xa, xbuf, da, db in (("k", xkb, None, DX_KA, DX_KB),
                                     ("v", xvb, None, DX_VA, DX_VB),
                                     ("r", xrb, None, DX_RA, DX_RB)):
            dst = {"k": xkb, "v": xvb, "r": xrb}[nm][p]
            def _shm(e, p=p, db=db):
                r = None
                for cc in range(CC):
                    r = e.tensor_scalar(shmb[:, cc, :], hb[p][:, cc, 0:W1T],
                                        dvec[:, cc, db:db + 1], None, OP.mult)
                return r
            b.step("dve", [f"hb.{p}", f"hbc.{p}", "dvec"], ["shmb"], _shm, tag=f"p1sh{nm}{i}")
            def _mix(e, p=p, da=da, dst=dst):
                r = None
                for cc in range(CC):
                    r = e.scalar_tensor_tensor(dst[:, cc, :], hb[p][:, cc, 1:W1T + 1],
                                               dvec[:, cc, da:da + 1], shmb[:, cc, :],
                                               OP.mult, OP.add)
                return r
            b.step("dve", [f"hb.{p}", "dvec", "shmb"], [f"x{nm}.{p}"], _mix, tag=f"p1mx{nm}{i}")

        for jn, (nm, wsb, xsb) in enumerate((("k", wk_sb, xkb), ("v", wv_sb, xvb),
                                             ("r", wr_sb, xrb))):
            for occ in range(CC):
                bank = (i * 24 + jn * 8 + occ) % 4
                def _mm(e, p=p, occ=occ, wsb=wsb, xsb=xsb, bank=bank):
                    r = None
                    for kc in range(CC):
                        r = e.matmul(ppj[bank][:, :W1T], wsb[:, kc, occ * 128:(occ + 1) * 128],
                                     xsb[p][:, kc, :], start=(kc == 0), stop=(kc == CC - 1))
                    return r
                b.step("pe", [f"x{nm}.{p}", nm], [f"ppj.{bank}"], _mm, tag=f"p1p{nm}{occ}_{i}")
                if nm == "k":
                    b.step("dve", [f"ppj.{bank}"], [f"kst.{p}"],
                           lambda e, p=p, occ=occ, bank=bank: e.tensor_copy(kst[p][:, occ, :], ppj[bank][:, :W1T]),
                           tag=f"p1ek{occ}_{i}")
                elif nm == "v":
                    b.step("dve", [f"ppj.{bank}"], [f"vst.{p}"],
                           lambda e, p=p, occ=occ, bank=bank: e.tensor_copy(vst[p][:, occ, :], ppj[bank][:, :W1T]),
                           tag=f"p1ev{occ}_{i}")
                else:
                    b.step("act", [f"ppj.{bank}", "consts"], [f"rst.{p}"],
                           lambda e, p=p, occ=occ, bank=bank: e.activation(
                               out=rst[p][:, occ, :], in_=ppj[bank][:, :W1T],
                               func=AF.Exp, scale=C_M1),
                           tag=f"p1er{occ}_{i}")
        def _rsig(e, p=p):
            e.activation(out=rst[p][:, :, :], in_=rst[p][:, :, :], func=AF.Ln, bias=C_ONE)
            return e.activation(out=rst[p][:, :, :], in_=rst[p][:, :, :], func=AF.Exp, scale=C_M1)
        b.step("act", [f"rst.{p}", "consts"], [f"rst.{p}"], _rsig, tag=f"p1rs{i}")

        b.step("sync", [f"kst.{p}"], ["ksd"],
               lambda e, t0=t0, p=p: e.dma_start(out=ks_d[:, :, t0:t0 + W1T], in_=kst[p][:, :, :]),
               q="dsy", tag=f"p1sk{i}")
        b.step("sync", [f"vst.{p}"], ["vsd"],
               lambda e, t0=t0, p=p: e.dma_start(out=vs_d[:, :, t0:t0 + W1T], in_=vst[p][:, :, :]),
               q="dsy", tag=f"p1sv{i}")
        b.step("sync", [f"rst.{p}"], ["rsd"],
               lambda e, t0=t0, p=p: e.dma_start(out=rs_d[:, :, t0:t0 + W1T], in_=rst[p][:, :, :]),
               q="dsy", tag=f"p1sr{i}")

    b.barrier()
    p01.close()

    p2 = p02
    kt = [p2.enter_context(nc.sbuf_tensor(f"kt{j}", [128, T], F32)) for j in range(2)]
    vt = [p2.enter_context(nc.sbuf_tensor(f"vt{j}", [128, T], F32)) for j in range(2)]
    rt = [p2.enter_context(nc.sbuf_tensor(f"rt{j}", [128, T], F32)) for j in range(2)]
    e1t = p2.enter_context(nc.sbuf_tensor("e1t", [128, T], F32))
    e2t = p2.enter_context(nc.sbuf_tensor("e2t", [128, T], F32))
    htt = p2.enter_context(nc.sbuf_tensor("htt", [128, T], F32))
    gtt = p2.enter_context(nc.sbuf_tensor("gtt", [128, T], F32))
    att = p2.enter_context(nc.sbuf_tensor("att", [128, T], F32))
    btt = p2.enter_context(nc.sbuf_tensor("btt", [128, T], F32))
    invt = p2.enter_context(nc.sbuf_tensor("invt", [128, T], F32))
    wkvt = p2.enter_context(nc.sbuf_tensor("wkvt", [128, T], F32))
    ewbt = p2.enter_context(nc.sbuf_tensor("ewbt", [128, T], F32))
    rwsb = p2.enter_context(nc.sbuf_tensor("rwsb", [128, CC, T], BF16))
    wost = [p2.enter_context(nc.sbuf_tensor(f"wost{j}", [128, 512], F32)) for j in range(2)]
    pw2 = [p2.enter_context(nc.psum_tensor(f"pw2{j}", [128, 512], F32)) for j in range(2)]

    for bb in range(NB):
        bT = bb * T
        for cc in range(CC):
            u = bb * CC + cc
            p = u % 2
            b.step("sync", [], [f"kt.{p}"],
                   lambda e, bT=bT, cc=cc, p=p: e.dma_start(out=kt[p][:, :], in_=ks_d[:, cc, bT:bT + T]),
                   q="dsy", tag=f"p2lk{u}")
            b.step("sync", [], [f"vt.{p}"],
                   lambda e, bT=bT, cc=cc, p=p: e.dma_start(out=vt[p][:, :], in_=vs_d[:, cc, bT:bT + T]),
                   q="dsy", tag=f"p2lv{u}")
            b.step("sync", [], [f"rt.{p}"],
                   lambda e, bT=bT, cc=cc, p=p: e.dma_start(out=rt[p][:, :], in_=rs_d[:, cc, bT:bT + T]),
                   q="dsy", tag=f"p2lr{u}")

            b.step("act", [f"kt.{p}", "vec"], ["e1t"],
                   lambda e, cc=cc, p=p: e.activation(out=e1t[:, :], in_=kt[p][:, :],
                                                      func=AF.Exp, bias=vec[:, cc, V_U:V_U + 1]),
                   tag=f"p2e1{u}")
            b.step("dve", ["e1t"], ["e1t"],
                   lambda e: e.tensor_scalar(e1t[:, :], e1t[:, :], EXP_CLIP_LO, EXP_CLIP_HI, OP.max, OP.min),
                   tag=f"p2c1{u}")
            b.step("act", [f"kt.{p}"], ["e2t"],
                   lambda e, p=p: e.activation(out=e2t[:, :], in_=kt[p][:, :], func=AF.Exp),
                   tag=f"p2e2{u}")
            b.step("dve", ["e2t"], ["e2t"],
                   lambda e: e.tensor_scalar(e2t[:, :], e2t[:, :], EXP_CLIP_LO, EXP_CLIP_HI, OP.max, OP.min),
                   tag=f"p2c2{u}")
            b.step("dve", ["e1t", f"vt.{p}"], ["htt"],
                   lambda e, p=p: e.tensor_tensor(htt[:, :], e1t[:, :], vt[p][:, :], OP.mult),
                   tag=f"p2h{u}")
            b.step("dve", ["e2t", f"vt.{p}"], ["gtt"],
                   lambda e, p=p: e.tensor_tensor(gtt[:, :], e2t[:, :], vt[p][:, :], OP.mult),
                   tag=f"p2g{u}")
            b.step("pool", ["gtt", "e2t", "ewbt"], [f"p2rl{u}"], lambda e: e.nop(), tag=f"p2rl{u}")
            b.step("pool", [f"kt.{p}", "dvec"], ["ewbt"],
                   lambda e, cc=cc, p=p: e.tensor_scalar(ewbt[:, :], kt[p][:, :], 0.0,
                                                         dvec[:, cc, DX_EW:DX_EW + 1], OP.mult, OP.add),
                   tag=f"p2ew{u}")
            b.step("dve", ["ewbt", "gtt", f"p2rl{u}"], ["att"],
                   lambda e: e.tensor_tensor_scan(att[:, :], ewbt[:, :], gtt[:, :], 0.0, OP.mult, OP.add),
                   tag="scan")
            b.step("dve", ["ewbt", "e2t", f"p2rl{u}"], ["btt"],
                   lambda e: e.tensor_tensor_scan(btt[:, :], ewbt[:, :], e2t[:, :], 0.0, OP.mult, OP.add),
                   tag="scan")
            b.step("pool", ["att", "btt"], [f"p2rs{u}"], lambda e: e.nop(), tag=f"p2rs{u}")
            def _num(e):
                e.tensor_copy(gtt[:, 0:1], htt[:, 0:1])
                return e.tensor_tensor(gtt[:, 1:T], att[:, 0:T - 1], htt[:, 1:T], OP.add)
            b.step("dve", ["att", "htt", f"p2rs{u}"], ["gtt"], _num, tag=f"p2n{u}")
            def _den(e):
                e.tensor_copy(e2t[:, 0:1], e1t[:, 0:1])
                return e.tensor_tensor(e2t[:, 1:T], btt[:, 0:T - 1], e1t[:, 1:T], OP.add)
            b.step("dve", ["btt", "e1t", f"p2rs{u}"], ["e2t"], _den, tag=f"p2d{u}")
            def _inv(e):
                e.activation(out=invt[:, :], in_=e2t[:, :], func=AF.Ln, bias=C_E8)
                return e.activation(out=invt[:, :], in_=invt[:, :], func=AF.Exp, scale=C_M1)
            b.step("act", ["e2t", "consts"], ["invt"], _inv, tag=f"p2i{u}")
            b.step("dve", ["gtt", "invt"], ["wkvt"],
                   lambda e: e.tensor_tensor(wkvt[:, :], gtt[:, :], invt[:, :], OP.mult),
                   tag=f"p2w{u}")
            b.step("dve", ["wkvt", f"rt.{p}"], ["rwsb"],
                   lambda e, cc=cc, p=p: e.tensor_tensor(rwsb[:, cc, :], wkvt[:, :], rt[p][:, :], OP.mult),
                   tag=f"p2rw{u}")

        for jt in range(T // 512):
            for occ in range(CC):
                pk = (jt * CC + occ) % 2
                def _wo(e, jt=jt, occ=occ, pk=pk):
                    r = None
                    for kc in range(CC):
                        r = e.matmul(pw2[pk][:, :], wo_sb[:, kc, occ * 128:(occ + 1) * 128],
                                     rwsb[:, kc, jt * 512:(jt + 1) * 512],
                                     start=(kc == 0), stop=(kc == CC - 1))
                    return r
                b.step("pe", ["rwsb", "wo"], [f"pw2.{pk}"], _wo, tag=f"p2wo{bb}_{jt}_{occ}")
                b.step("dve", [f"pw2.{pk}"], [f"wost.{pk}"],
                       lambda e, pk=pk: e.tensor_copy(wost[pk][:, :], pw2[pk][:, :]),
                       tag=f"p2we{bb}_{jt}_{occ}")
                b.step("sync", [f"wost.{pk}"], ["wod"],
                       lambda e, bT=bT, jt=jt, occ=occ, pk=pk: e.dma_start(
                           out=wo_d[:, occ, bT + jt * 512:bT + (jt + 1) * 512], in_=wost[pk][:, :]),
                       q="dsy", tag=f"p2ws{bb}_{jt}_{occ}")

    b.barrier()
    p2.close()

    xk2_d = scratch("xk2_s", [128, CC, NTOK], BF16)
    r2_d = scratch("r2_s", [128, CC, NTOK])
    y_d = scratch("y_s", [128, CC, NTOK])

    W3A = 512
    N3A = NTOK // W3A
    p3 = contextlib.ExitStack()
    wr2_sb = p3.enter_context(nc.sbuf_tensor("wr2_sb", [128, CC, C], BF16))
    b.step("pool", [], ["wr2"], lambda e: e.dma_start(out=wr2_sb[:, :, :], in_=wrec2.rearrange("(kc p) m -> p kc m", p=128)), q="dgp", tag="ldwr2")

    wvo3 = p3.enter_context(nc.sbuf_tensor("wvo30", [128, CC, W3A], F32))
    x0c = p3.enter_context(nc.sbuf_tensor("x0c0", [128, CC, W3A], F32))
    t3a = p3.enter_context(nc.sbuf_tensor("t3a", [128, CC, W3A], F32))
    t3b = p3.enter_context(nc.sbuf_tensor("t3b", [128, CC, W3A], F32))
    xac = p3.enter_context(nc.sbuf_tensor("xac", [128, CC, W3A], F32))
    h2b = [p3.enter_context(nc.sbuf_tensor(f"h2b{j}", [128, CC, W3A + 1], F32)) for j in range(2)]
    shm2 = p3.enter_context(nc.sbuf_tensor("shm2", [128, CC, W3A], F32))
    xk2b = p3.enter_context(nc.sbuf_tensor("xk2b0", [128, CC, W3A], BF16))
    xr2b = p3.enter_context(nc.sbuf_tensor("xr2b0", [128, CC, W3A], BF16))
    r2c = p3.enter_context(nc.sbuf_tensor("r2c", [128, CC, W3A], F32))
    alc = p3.enter_context(nc.sbuf_tensor("alc", [128, W3A], F32))
    s3c = p3.enter_context(nc.sbuf_tensor("s3c", [128, W3A], F32))
    pstA = p3.enter_context(nc.psum_tensor("pstA", [128, 512], F32))
    pstB = p3.enter_context(nc.psum_tensor("pstB", [128, 512], F32))
    pr2 = [p3.enter_context(nc.psum_tensor(f"pr2{j}", [128, 512], F32)) for j in range(2)]

    TPBA = T // W3A

    for i in range(N3A):
        t0 = i * W3A
        p = i % 2

        b.step("sync", [], ["wvo3"],
               lambda e, t0=t0: e.dma_start(out=wvo3[:, :, :], in_=wo_d[:, :, t0:t0 + W3A]),
               q="dsy", tag=f"p3lw{i}")
        b.step("sync", [], ["x0c"],
               lambda e, t0=t0: e.dma_start(
                   out=x0c[:, :, :],
                   in_=x0f[:, t0:t0 + W3A].rearrange("(cc q) t -> q cc t", q=128)),
               q="dsy", tag=f"p3lx{i}")

        def _mean(e):
            r = None
            for kc in range(CC):
                r = e.matmul(pstA[:, :], onesC, x0c[:, kc, :],
                             start=(kc == 0), stop=(kc == CC - 1))
            return r
        b.step("pe", ["x0c", "consts"], ["pstA"], _mean, tag=f"p3mean{i}")
        b.step("dve", ["x0c", "pstA"], ["t3a"],
               lambda e: e.tensor_tensor(t3a[:, :, :], x0c[:, :, :],
                                         bc(pstA[:, :], CC), OP.subtract),
               tag=f"p3z{i}")
        b.step("act", ["t3a"], ["t3b"],
               lambda e: e.activation(out=t3b[:, :, :], in_=t3a[:, :, :], func=AF.Square),
               tag=f"p3sq{i}")
        def _var(e):
            r = None
            for kc in range(CC):
                r = e.matmul(pstB[:, :], onesC, t3b[:, kc, :],
                             start=(kc == 0), stop=(kc == CC - 1))
            return r
        b.step("pe", ["t3b", "consts"], ["pstB"], _var, tag=f"p3var{i}")
        def _alpha(e):
            e.activation(out=alc[:, :], in_=pstB[:, :], func=AF.Ln, bias=C_E5)
            return e.activation(out=alc[:, :], in_=alc[:, :], func=AF.Exp, scale=C_MH)
        b.step("act", ["pstB", "consts"], ["alc"], _alpha, tag=f"p3al{i}")
        b.step("dve", ["t3a", "alc"], ["t3b"],
               lambda e: e.tensor_tensor(t3b[:, :, :], t3a[:, :, :], bc(alc[:, :], CC), OP.mult),
               tag=f"p3ln0{i}")
        b.step("dve", ["t3b", "wvo3"], ["xac"],
               lambda e: e.tensor_tensor(xac[:, :, :], t3b[:, :, :], wvo3[:, :, :], OP.add),
               tag=f"p3xa{i}")
        b.step("dve", ["x0c", "xac"], ["t3b"],
               lambda e: e.tensor_tensor(t3b[:, :, :], x0c[:, :, :], xac[:, :, :], OP.add),
               tag=f"p3y{i}")
        b.step("sync", ["t3b"], ["yd"],
               lambda e, t0=t0: e.dma_start(out=y_d[:, :, t0:t0 + W3A], in_=t3b[:, :, :]),
               q="dsy", tag=f"p3sy{i}")
        b.step("act", ["xac"], ["t3a"],
               lambda e: e.activation(out=t3a[:, :, :], in_=xac[:, :, :], func=AF.Square),
               tag=f"p3sq2{i}")
        def _ms2(e):
            r = None
            for kc in range(CC):
                r = e.matmul(pstA[:, :], onesC, t3a[:, kc, :],
                             start=(kc == 0), stop=(kc == CC - 1))
            return r
        b.step("pe", ["t3a", "consts"], ["pstA"], _ms2, tag=f"p3ms2{i}")
        def _s3(e):
            e.activation(out=s3c[:, :], in_=pstA[:, :], func=AF.Ln, bias=C_E6)
            return e.activation(out=s3c[:, :], in_=s3c[:, :], func=AF.Exp, scale=C_MH)
        b.step("act", ["pstA", "consts"], ["s3c"], _s3, tag=f"p3s3{i}")
        b.step("dve", ["xac", "s3c"], [f"h2.{p}"],
               lambda e, p=p: e.tensor_tensor(h2b[p][:, :, 1:W3A + 1], xac[:, :, :],
                                              bc(s3c[:, :], CC), OP.mult),
               tag=f"p3h2{i}")
        if i % TPBA == 0:
            b.step("dve", [], [f"h2c.{p}"],
                   lambda e, p=p: e.memset(h2b[p][:, :, 0:1], 0.0), tag=f"p3bz{i}")
        else:
            b.step("dve", [f"h2.{1 - p}"], [f"h2c.{p}"],
                   lambda e, p=p: e.tensor_copy(h2b[p][:, :, 0:1], h2b[1 - p][:, :, W3A:W3A + 1]),
                   tag=f"p3bc{i}")

        for nm, da, db in (("k2", DX_K2A, DX_K2B), ("r2", DX_R2A, DX_R2B)):
            dst = {"k2": xk2b, "r2": xr2b}[nm]
            def _shm(e, p=p, db=db):
                r = None
                for cc in range(CC):
                    r = e.tensor_scalar(shm2[:, cc, :], h2b[p][:, cc, 0:W3A],
                                        dvec[:, cc, db:db + 1], None, OP.mult)
                return r
            b.step("dve", [f"h2.{p}", f"h2c.{p}", "dvec"], ["shm2"], _shm, tag=f"p3sh{nm}{i}")
            def _mix(e, p=p, da=da, dst=dst):
                r = None
                for cc in range(CC):
                    r = e.scalar_tensor_tensor(dst[:, cc, :], h2b[p][:, cc, 1:W3A + 1],
                                               dvec[:, cc, da:da + 1], shm2[:, cc, :],
                                               OP.mult, OP.add)
                return r
            b.step("dve", [f"h2.{p}", "dvec", "shm2"], [f"x{nm}b"], _mix, tag=f"p3mx{nm}{i}")
        b.step("sync", ["xk2b"], ["xk2d"],
               lambda e, t0=t0: e.dma_start(out=xk2_d[:, :, t0:t0 + W3A], in_=xk2b[:, :, :]),
               q="dsy", tag=f"p3sxk2{i}")

        for occ in range(CC):
            pk = occ % 2
            def _r2mm(e, occ=occ, pk=pk):
                r = None
                for kc in range(CC):
                    r = e.matmul(pr2[pk][:, :], wr2_sb[:, kc, occ * 128:(occ + 1) * 128],
                                 xr2b[:, kc, :], start=(kc == 0), stop=(kc == CC - 1))
                return r
            b.step("pe", ["xr2b", "wr2"], [f"pr2.{pk}"], _r2mm, tag=f"p3r2m{occ}_{i}")
            b.step("act", [f"pr2.{pk}", "consts"], ["r2c"],
                   lambda e, occ=occ, pk=pk: e.activation(
                       out=r2c[:, occ, :], in_=pr2[pk][:, :], func=AF.Exp, scale=C_M1),
                   tag=f"p3r2e{occ}_{i}")
        def _r2sig(e):
            e.activation(out=r2c[:, :, :], in_=r2c[:, :, :], func=AF.Ln, bias=C_ONE)
            return e.activation(out=r2c[:, :, :], in_=r2c[:, :, :], func=AF.Exp, scale=C_M1)
        b.step("act", ["r2c", "consts"], ["r2c"], _r2sig, tag=f"p3r2s{i}")
        b.step("sync", ["r2c"], ["r2d"],
               lambda e, t0=t0: e.dma_start(out=r2_d[:, :, t0:t0 + W3A], in_=r2c[:, :, :]),
               q="dsy", tag=f"p3sr2{i}")

    b.barrier()
    p3.close()

    W3B = 256
    N3B = NTOK // W3B
    p3b = contextlib.ExitStack()
    wk2_sb = p3b.enter_context(nc.sbuf_tensor("wk2_sb", [128, CC, I_FF], BF16))
    wv2_sb = p3b.enter_context(nc.sbuf_tensor("wv2_sb", [128, IC, C], BF16))
    b.step("pool", [], ["wk2"], lambda e: e.dma_start(out=wk2_sb[:, :, :], in_=wkey2.rearrange("(kc p) m -> p kc m", p=128)), q="dgp", tag="ldwk2")
    b.step("pool", [], ["wv2"], lambda e: e.dma_start(out=wv2_sb[:, :, :], in_=wval2.rearrange("(kc p) m -> p kc m", p=128)), q="dgp", tag="ldwv2")

    xkl = [p3b.enter_context(nc.sbuf_tensor(f"xkl{j}", [128, CC, W3B], BF16)) for j in range(2)]
    yl = [p3b.enter_context(nc.sbuf_tensor("yl0", [128, CC, W3B], F32))] * 2
    r2l = [p3b.enter_context(nc.sbuf_tensor("r2l0", [128, CC, W3B], F32))] * 2
    k2f = p3b.enter_context(nc.sbuf_tensor("k2f", [128, IC, W3B], BF16))
    kq = p3b.enter_context(nc.sbuf_tensor("kq", [128, 512], F32))
    x1c = p3b.enter_context(nc.sbuf_tensor("x1c", [128, CC, W3B], F32))
    tb3 = p3b.enter_context(nc.sbuf_tensor("tb3", [128, CC, W3B], F32))
    tb4 = p3b.enter_context(nc.sbuf_tensor("tb4", [128, CC, W3B], F32))
    xfb = p3b.enter_context(nc.sbuf_tensor("xfb", [128, CC, W3B], BF16))
    a4c = p3b.enter_context(nc.sbuf_tensor("a4c", [128, W3B], F32))
    ltst = p3b.enter_context(nc.sbuf_tensor("ltst", [8, W3B], F32))
    pk2 = [p3b.enter_context(nc.psum_tensor(f"pk2{j}", [128, 512], F32)) for j in range(2)]
    pcm = p3b.enter_context(nc.psum_tensor("pcm", [128, 1024], F32))
    pst2 = p3b.enter_context(nc.psum_tensor("pst2", [128, 512], F32))
    ptt = [p3b.enter_context(nc.psum_tensor(f"ptt{j}", [128, 512], F32)) for j in range(2)]

    for i in range(N3B):
        t0 = i * W3B
        p = i % 2
        b.step("sync", ["xk2d"], [f"xkl.{p}"],
               lambda e, t0=t0, p=p: e.dma_start(out=xkl[p][:, :, :], in_=xk2_d[:, :, t0:t0 + W3B]),
               q="dsy", tag=f"p3blx{i}")
        b.step("sync", ["yd"], ["yl"],
               lambda e, t0=t0, p=p: e.dma_start(out=yl[p][:, :, :], in_=y_d[:, :, t0:t0 + W3B]),
               q="dsy", tag=f"p3bly{i}")
        b.step("sync", ["r2d"], ["r2l"],
               lambda e, t0=t0, p=p: e.dma_start(out=r2l[p][:, :, :], in_=r2_d[:, :, t0:t0 + W3B]),
               q="dsy", tag=f"p3blr{i}")

        for g in range(IC // 2):
            def _k2mm(e, p=p, g=g):
                r = None
                for fl in range(2):
                    fc = 2 * g + fl
                    for kc in range(CC):
                        r = e.matmul(pk2[g % 2][:, fl * W3B:(fl + 1) * W3B],
                                     wk2_sb[:, kc, fc * 128:(fc + 1) * 128],
                                     xkl[p][:, kc, :],
                                     start=(kc == 0 and fl == 0), stop=(kc == CC - 1))
                    if fl == 0:
                        pass
                return r
            b.step("pe", [f"xkl.{p}", "wk2"], [f"pk2.{g % 2}"], _k2mm, tag=f"p3k2m{g}_{i}")
            b.step("dve", [f"pk2.{g % 2}"], ["kq"],
                   lambda e, g=g: e.tensor_copy(kq[:, :], pk2[g % 2][:, :]), tag=f"p3k2e{g}_{i}")
            b.step("dve", ["kq"], ["k2f"],
                   lambda e, g=g: e.scalar_tensor_tensor(
                       k2f[:, :, :].rearrange("q fc t -> q (fc t)")[:, 2 * g * W3B:(2 * g + 2) * W3B],
                       kq[:, :], 0.0, kq[:, :], OP.max, OP.mult),
                   tag=f"p3k2r{g}_{i}")
        for ch in range(2):
            def _cmm(e, ch=ch):
                r = None
                for fc in range(IC):
                    for ol in range(4):
                        occ = 4 * ch + ol
                        r = e.matmul(pcm[:, ol * W3B:(ol + 1) * W3B],
                                     wv2_sb[:, fc, occ * 128:(occ + 1) * 128],
                                     k2f[:, fc, :],
                                     start=(fc == 0 and ol % 2 == 0),
                                     stop=(fc == IC - 1))
                return r
            b.step("pe", ["k2f", "wv2"], ["pcm"], _cmm, tag=f"p3cm{ch}_{i}")
            def _x1h(e, ch=ch, p=p):
                e.tensor_tensor(tb3[:, 4 * ch:4 * ch + 4, :], r2l[p][:, 4 * ch:4 * ch + 4, :],
                                pcm[:, :].rearrange("q (ol t) -> q ol t", ol=4), OP.mult)
                return e.tensor_tensor(x1c[:, 4 * ch:4 * ch + 4, :], yl[p][:, 4 * ch:4 * ch + 4, :],
                                       tb3[:, 4 * ch:4 * ch + 4, :], OP.add)
            b.step("dve", ["pcm", "r2l", "yl"], ["x1c", "tb3"], _x1h, tag=f"p3x1{ch}_{i}")

        def _m4(e):
            r = None
            for kc in range(CC):
                r = e.matmul(pst2[:, 0:W3B], onesC, x1c[:, kc, :],
                             start=(kc == 0), stop=(kc == CC - 1))
            return r
        b.step("pe", ["x1c", "consts"], ["pst2"], _m4, tag=f"p3m4{i}")
        b.step("dve", ["x1c", "pst2"], ["tb3"],
               lambda e: e.tensor_tensor(tb3[:, :, :], x1c[:, :, :],
                                         bc(pst2[:, 0:W3B], CC), OP.subtract),
               tag=f"p3z4{i}")
        b.step("act", ["tb3"], ["tb4"],
               lambda e: e.activation(out=tb4[:, :, :], in_=tb3[:, :, :], func=AF.Square),
               tag=f"p3sq4{i}")
        def _v4(e):
            r = None
            for kc in range(CC):
                r = e.matmul(pst2[:, 256:256 + W3B], onesC, tb4[:, kc, :],
                             start=(kc == 0), stop=(kc == CC - 1))
            return r
        b.step("pe", ["tb4", "consts"], ["pst2"], _v4, tag=f"p3v4{i}")
        def _a4(e):
            e.activation(out=a4c[:, :], in_=pst2[:, 256:256 + W3B], func=AF.Ln, bias=C_E5)
            return e.activation(out=a4c[:, :], in_=a4c[:, :], func=AF.Exp, scale=C_MH)
        b.step("act", ["pst2", "consts"], ["a4c"], _a4, tag=f"p3a4{i}")
        b.step("dve", ["tb3", "a4c"], ["tb4"],
               lambda e: e.tensor_tensor(tb4[:, :, :], tb3[:, :, :], bc(a4c[:, :], CC), OP.mult),
               tag=f"p3xf{i}")
        b.step("dve", ["x1c", "tb4"], ["x1c"],
               lambda e: e.tensor_tensor(x1c[:, :, :], x1c[:, :, :], tb4[:, :, :], OP.add),
               tag=f"p3s{i}")
        b.step("dve", ["tb4"], ["xfb"],
               lambda e: e.tensor_copy(xfb[:, :, :], tb4[:, :, :]), tag=f"p3xfb{i}")

        def _lg(e):
            r = None
            for kc in range(CC):
                r = e.matmul(pst2[0:8, 256:256 + W3B], routersb[:, kc, :], xfb[:, kc, :],
                             start=(kc == 0), stop=(kc == CC - 1))
            return r
        b.step("pe", ["xfb", "routersb"], ["pst2"], _lg, tag=f"p3lg{i}")
        b.step("dve", ["pst2"], ["ltst"],
               lambda e: e.tensor_copy(ltst[:, :], pst2[0:8, 256:256 + W3B]), tag=f"p3lge{i}")
        for half in range(2):
            b.step("pe", ["ltst", "consts"], ["ptt.1"],
                   lambda e, half=half: e.transpose(ptt[1][:, half * 8:half * 8 + 8],
                                                    ltst[:, half * 128:(half + 1) * 128],
                                                    ident[0:8, 0:8]),
                   tag=f"p3ltt{half}_{i}")
        b.step("dve", ["ptt.1"], ["ltall"],
               lambda e, i=i: e.tensor_copy(
                   lt_all[:, 2 * i:2 * i + 2, :],
                   ptt[1][:, 0:16].rearrange("q (h e) -> q h e", h=2)),
               tag=f"p3lte{i}")

        for half in range(2):
            def _txf(e, half=half):
                r = None
                pv = ptt[0][:, :].bitcast(BF16)
                for cc in range(CC):
                    r = e.transpose(pv[:, cc * 128:(cc + 1) * 128],
                                    xfb[:, cc, half * 128:half * 128 + 128], identbf[:, :])
                return r
            b.step("pe", ["xfb", "identbf"], ["ptt.0"], _txf, tag=f"p3txf{half}_{i}")
            b.step("dve", ["ptt.0"], ["kq"],
                   lambda e: e.tensor_copy(kq[:, :].bitcast(BF16)[:, 0:C],
                                           ptt[0][:, :].bitcast(BF16)[:, 0:C]),
                   tag=f"p3txfe{half}_{i}")
            b.step("sync", ["kq"], ["xfd"],
                   lambda e, t0=t0, half=half: e.dma_start(
                       out=xf_d[t0 + half * 128:t0 + half * 128 + 128, :],
                       in_=kq[:, :].bitcast(BF16)[:, 0:C]),
                   q="dsy", tag=f"p3sxf{half}_{i}")
        for half in range(2):
            for sub in range(2):
                def _ts(e, half=half, sub=sub):
                    r = None
                    for j in range(4):
                        cc = sub * 4 + j
                        r = e.transpose(ptt[1][:, j * 128:(j + 1) * 128],
                                        x1c[:, cc, half * 128:half * 128 + 128], ident)
                    return r
                b.step("pe", ["x1c", "consts"], ["ptt.1"], _ts, tag=f"p3ts{half}_{sub}_{i}")
                b.step("dve", ["ptt.1"], ["tb3"],
                       lambda e, sub=sub: e.tensor_copy(
                           tb3[:, :, :].rearrange("q cc t -> q (cc t)")[:, sub * 512:(sub + 1) * 512],
                           ptt[1][:, :]),
                       tag=f"p3tse{half}_{sub}_{i}")
            b.step("sync", ["tb3"], ["sd"],
                   lambda e, t0=t0, half=half: e.dma_start(
                       out=s_d[t0 + half * 128:t0 + half * 128 + 128, :],
                       in_=tb3[:, :, :].rearrange("q cc t -> q (cc t)")[:, 0:C]),
                   q="dsy", tag=f"p3ss{half}_{i}")

    b.barrier()
    p3b.close()

    p4 = contextlib.ExitStack()
    m1c = p4.enter_context(nc.sbuf_tensor("m1c", [128, 1], F32))
    m2c = p4.enter_context(nc.sbuf_tensor("m2c", [128, 1], F32))
    dcol = p4.enter_context(nc.sbuf_tensor("dcol", [128, 1], F32))
    tcol = p4.enter_context(nc.sbuf_tensor("tcol", [128, 1], F32))
    tcol2 = p4.enter_context(nc.sbuf_tensor("tcol2", [128, 1], F32))
    eqc = p4.enter_context(nc.sbuf_tensor("eqc", [128, NEXP], F32))
    mic = p4.enter_context(nc.sbuf_tensor("mic", [128, NEXP], F32))
    ltc2 = p4.enter_context(nc.sbuf_tensor("ltc2", [128, NEXP], F32))
    o1c = p4.enter_context(nc.sbuf_tensor("o1c", [128, NEXP], F32))
    o2c = p4.enter_context(nc.sbuf_tensor("o2c", [128, NEXP], F32))
    occ_ = p4.enter_context(nc.sbuf_tensor("occ_", [128, NEXP], F32))
    scc = p4.enter_context(nc.sbuf_tensor("scc", [128, NEXP], F32))
    bsc = p4.enter_context(nc.sbuf_tensor("bsc", [128, NEXP], F32))
    scf = p4.enter_context(nc.sbuf_tensor("scf", [128, NEXP], F32))
    tm8 = p4.enter_context(nc.sbuf_tensor("tm8", [128, NEXP], F32))
    cntr = p4.enter_context(nc.sbuf_tensor("cntr", [1, NEXP], F32))
    ccsb = p4.enter_context(nc.sbuf_tensor("ccsb", [NEXP, NEXP], F32))
    offr = p4.enter_context(nc.sbuf_tensor("offr", [1, NEXP], F32))
    offb = p4.enter_context(nc.sbuf_tensor("offb", [128, NEXP], F32))
    xft = [p4.enter_context(nc.sbuf_tensor(f"xft{j}", [128, C], BF16)) for j in range(2)]
    pS4 = p4.enter_context(nc.psum_tensor("pS4", [128, 512], F32))
    pBase = p4.enter_context(nc.psum_tensor("pBase", [128, 512], F32))
    pOff4 = p4.enter_context(nc.psum_tensor("pOff4", [128, 512], F32))

    AX = mybir.AxisListType.X

    m1w = p4.enter_context(nc.sbuf_tensor("m1w", [128, N3T], F32))
    m2w = p4.enter_context(nc.sbuf_tensor("m2w", [128, N3T], F32))
    tm32a = p4.enter_context(nc.sbuf_tensor("tm32a", [128, N3T], F32))
    tm32b = p4.enter_context(nc.sbuf_tensor("tm32b", [128, N3T], F32))
    tm32c = p4.enter_context(nc.sbuf_tensor("tm32c", [128, N3T], F32))
    eqw = p4.enter_context(nc.sbuf_tensor("eqw", [128, N3T, NEXP], F32))
    miw = p4.enter_context(nc.sbuf_tensor("miw", [128, N3T, NEXP], F32))
    o1w = p4.enter_context(nc.sbuf_tensor("o1w", [128, N3T, NEXP], F32))
    o2w = p4.enter_context(nc.sbuf_tensor("o2w", [128, N3T, NEXP], F32))
    oww = p4.enter_context(nc.sbuf_tensor("oww", [128, N3T, NEXP], F32))
    ltc2w = p4.enter_context(nc.sbuf_tensor("ltc2w", [128, N3T, NEXP], F32))
    sccall = p4.enter_context(nc.sbuf_tensor("sccall", [128, N3T, NEXP], F32))
    bscall = p4.enter_context(nc.sbuf_tensor("bscall", [128, N3T, NEXP], F32))
    tsc4 = p4.enter_context(nc.sbuf_tensor("tsc4", [128, N3T, NEXP], F32))

    def bc3(ap2):
        return ap2.unsqueeze(2).broadcast_to([128, N3T, NEXP])

    iotaB = iota8.unsqueeze(1).broadcast_to([128, N3T, NEXP])

    _relay_state = {"tok": None, "n": 0}

    def dstep(reads, writes, emit, tag):
        r = list(reads) + ([_relay_state["tok"]] if _relay_state["tok"] else [])
        b.step("dve", r, writes, emit, tag=tag)
        tok = f"_p4tok{_relay_state['n']}"
        _relay_state["n"] += 1
        b.step("pool", list(writes), [tok], lambda e: e.nop(), tag=f"relay_{tag}")
        _relay_state["tok"] = tok

    dstep(["ltall"], ["m1w"],
           lambda e: e.tensor_reduce(m1w[:, :], lt_all[:, :, :], axis=AX, op=OP.max), tag="p4m1")
    dstep(["ltall", "m1w"], ["eqw"],
           lambda e: e.tensor_tensor(eqw[:, :, :], lt_all[:, :, :], bc3(m1w[:, :]), OP.is_equal), tag="p4eq1")
    dstep(["eqw"], ["tsc4"],
           lambda e: e.tensor_scalar(tsc4[:, :, :], eqw[:, :, :], -1000.0, None, OP.mult), tag="p4mi1a")
    dstep(["tsc4", "consts"], ["miw"],
           lambda e: e.tensor_tensor(miw[:, :, :], tsc4[:, :, :], iotaB, OP.add), tag="p4mi1")
    dstep(["miw"], ["tm32a"],
           lambda e: e.tensor_reduce(tm32a[:, :], miw[:, :, :], axis=AX, op=OP.min), tag="p4rm1")
    dstep(["tm32a"], ["e1a"],
           lambda e: e.tensor_scalar(e1a[:, :], tm32a[:, :], 1000.0, None, OP.add), tag="p4e1")
    dstep(["e1a", "consts"], ["o1w"],
           lambda e: e.tensor_tensor(o1w[:, :, :], iotaB, bc3(e1a[:, :]), OP.is_equal), tag="p4o1")
    dstep(["o1w"], ["tsc4"],
           lambda e: e.tensor_scalar(tsc4[:, :, :], o1w[:, :, :], -1.0e9, None, OP.mult), tag="p4lt2a")
    dstep(["tsc4", "ltall"], ["ltc2w"],
           lambda e: e.tensor_tensor(ltc2w[:, :, :], tsc4[:, :, :], lt_all[:, :, :], OP.add), tag="p4lt2")
    dstep(["ltc2w"], ["m2w"],
           lambda e: e.tensor_reduce(m2w[:, :], ltc2w[:, :, :], axis=AX, op=OP.max), tag="p4m2")
    dstep(["ltc2w", "m2w"], ["eqw"],
           lambda e: e.tensor_tensor(eqw[:, :, :], ltc2w[:, :, :], bc3(m2w[:, :]), OP.is_equal), tag="p4eq2")
    dstep(["eqw"], ["tsc4"],
           lambda e: e.tensor_scalar(tsc4[:, :, :], eqw[:, :, :], -1000.0, None, OP.mult), tag="p4mi2a")
    dstep(["tsc4", "consts"], ["miw"],
           lambda e: e.tensor_tensor(miw[:, :, :], tsc4[:, :, :], iotaB, OP.add), tag="p4mi2")
    dstep(["miw"], ["tm32a"],
           lambda e: e.tensor_reduce(tm32a[:, :], miw[:, :, :], axis=AX, op=OP.min), tag="p4rm2")
    b.step("pool", ["tm32a"], ["e2a"],
           lambda e: e.tensor_scalar(e2a[:, :], tm32a[:, :], 1000.0, None, OP.add), tag="p4e2")
    dstep(["m1w", "m2w"], ["tm32a"],
           lambda e: e.tensor_tensor(tm32a[:, :], m1w[:, :], m2w[:, :], OP.subtract), tag="p4dm")
    def _g1(e):
        e.activation(out=g1a[:, :], in_=tm32a[:, :], func=AF.Exp, scale=C_M1)
        e.activation(out=g1a[:, :], in_=g1a[:, :], func=AF.Ln, bias=C_ONE)
        return e.activation(out=g1a[:, :], in_=g1a[:, :], func=AF.Exp, scale=C_M1)
    b.step("act", ["tm32a", "consts"], ["g1a"], _g1, tag="p4g1")
    dstep(["g1a"], ["g2a"],
           lambda e: e.tensor_scalar(g2a[:, :], g1a[:, :], -1.0, 1.0, OP.mult, OP.add), tag="p4g2")
    dstep(["e2a", "consts"], ["o2w"],
           lambda e: e.tensor_tensor(o2w[:, :, :], iotaB, bc3(e2a[:, :]), OP.is_equal), tag="p4o2")
    b.step("pool", ["o1w", "o2w"], ["oww"],
           lambda e: e.tensor_tensor(oww[:, :, :], o1w[:, :, :], o2w[:, :, :], OP.add), tag="p4ow")
    b.step("pool", [], ["bscall"],
           lambda e: e.memset(bscall[:, 0, :], 0.0), tag="p4bz")

    for i in range(N3T):
        b.step("pe", ["oww", "consts"], ["pS4"],
               lambda e, i=i: e.matmul(pS4[:, 0:NEXP], LT128, oww[:, i, :], start=True, stop=True),
               tag=f"p4sc{i}")
        b.step("act", ["pS4"], ["sccall"],
               lambda e, i=i: e.copy(sccall[:, i, :], pS4[:, 0:NEXP]), tag=f"p4sce{i}")
        if i > 0:
            b.step("act", ["pBase"], ["bscall"],
                   lambda e, i=i: e.copy(bscall[:, i, :], pBase[:, 0:NEXP]), tag=f"p4be{i}")
        b.step("pe", ["oww", "consts", "bscall"], ["pBase"],
               lambda e, i=i: e.matmul(pBase[:, 0:NEXP], ones128, oww[:, i, :],
                                       start=(i == 0), stop=(i == N3T - 1)),
               tag=f"p4ba{i}")

    dstep(["sccall", "bscall"], ["eqw"],
           lambda e: e.tensor_tensor(eqw[:, :, :], sccall[:, :, :], bscall[:, :, :], OP.add), tag="p4scf")
    b.step("pool", ["eqw", "o1w"], ["miw"],
           lambda e: e.tensor_tensor(miw[:, :, :], eqw[:, :, :], o1w[:, :, :], OP.mult), tag="p4rs1")
    dstep(["miw"], ["r1a"],
           lambda e: e.tensor_reduce(r1a[:, :], miw[:, :, :], axis=AX, op=OP.add), tag="p4r1")
    b.step("pool", ["eqw", "o2w"], ["miw"],
           lambda e: e.tensor_tensor(miw[:, :, :], eqw[:, :, :], o2w[:, :, :], OP.mult), tag="p4rs2")
    dstep(["miw"], ["r2a"],
           lambda e: e.tensor_reduce(r2a[:, :], miw[:, :, :], axis=AX, op=OP.add), tag="p4r2")

    b.step("act", ["pBase"], ["cntr"],
           lambda e: e.copy(cntr[:, :], pBase[0:1, 0:NEXP]), tag="p4cnt")
    b.step("sync", ["cntr"], ["ccin"],
           lambda e: e.dma_start(out=ccin_d[:, :], in_=cntr[:, :]), q="dsy", tag="p4ccs")
    if USE_COLLECTIVE:
        b.step("pool", ["ccin"], ["ccout"],
               lambda e: e.collective_compute(
                   "AllGather", OP.bypass, replica_groups=[list(range(NCORES))],
                   ins=[ccin_d[:, :]], outs=[ccout_d[:, :]]),
               tag="p4cc")
        b.step("pool", ["ccout"], ["ccsb"],
               lambda e: e.dma_start(out=ccsb[:, :], in_=ccout_d[:, :]), q="dgp", tag="p4ccl")
    else:
        dstep(["ccin"], ["ccsb"],
               lambda e: e.memset(ccsb[:, :], 0.0), tag="p4cc0")
    b.step("pe", ["ccsb", "ltsb"], ["pOff4"],
           lambda e: e.matmul(pOff4[0:1, 0:NEXP], ltsb[:, :], ccsb[:, :], start=True, stop=True),
           tag="p4off")
    b.step("act", ["pOff4"], ["offr"],
           lambda e: e.copy(offr[:, :], pOff4[0:1, 0:NEXP]), tag="p4offe")
    b.step("pe", ["offr", "consts"], ["pOff4"],
           lambda e: e.matmul(pOff4[:, 8:8 + NEXP], ones128[0:1, :], offr[:, :],
                              start=True, stop=True),
           tag="p4offb")
    dstep(["pOff4"], ["offb"],
           lambda e: e.tensor_copy(offb[:, :], pOff4[:, 8:8 + NEXP]), tag="p4offc")

    offbB = offb[:, :].unsqueeze(1).broadcast_to([128, N3T, NEXP])
    for side, ow_s, ra, ga, gts, sli, ea in (
            (1, o1w, r1a, g1a, gts1, sl1i, e1a), (2, o2w, r2a, g2a, gts2, sl2i, e2a)):
        dstep(["offb", f"o{side}w"], ["miw"],
               lambda e, ow_s=ow_s: e.tensor_tensor(miw[:, :, :], offbB, ow_s[:, :, :], OP.mult),
               tag=f"p4of{side}")
        dstep(["miw"], ["tm32a"],
               lambda e: e.tensor_reduce(tm32a[:, :], miw[:, :, :], axis=AX, op=OP.add),
               tag=f"p4ofr{side}")
        b.step("pool", ["tm32a", f"r{side}a"], ["tm32b"],
               lambda e, ra=ra: e.tensor_tensor(tm32b[:, :], tm32a[:, :], ra[:, :], OP.add),
               tag=f"p4gl{side}")
        dstep(["tm32b"], ["tm32a"],
               lambda e: e.tensor_scalar(tm32a[:, :], tm32b[:, :], float(CAPACITY), None, OP.is_lt),
               tag=f"p4k{side}")
        b.step("pool", [f"r{side}a"], ["tm32c"],
               lambda e, ra=ra: e.tensor_scalar(tm32c[:, :], ra[:, :], float(SLOTCAP), None, OP.is_lt),
               tag=f"p4kl{side}")
        dstep(["tm32a", "tm32c"], ["tm32b"],
               lambda e: e.tensor_tensor(tm32b[:, :], tm32a[:, :], tm32c[:, :], OP.mult),
               tag=f"p4kp{side}")
        b.step("pool", ["tm32b", f"g{side}a"], [f"gts{side}"],
               lambda e, ga=ga, gts=gts: e.tensor_tensor(gts[:, :], tm32b[:, :], ga[:, :], OP.mult),
               tag=f"p4gt{side}")
        dstep([f"r{side}a"], ["tm32a"],
               lambda e, ra=ra: e.tensor_scalar(tm32a[:, :], ra[:, :], float(SLOTCAP - 1), None, OP.min),
               tag=f"p4sm{side}")
        dstep([f"e{side}a", "tm32a"], ["tm32b"],
               lambda e, ea=ea: e.scalar_tensor_tensor(tm32b[:, :], ea[:, :], float(SLOTCAP),
                                                       tm32a[:, :], OP.mult, OP.add),
               tag=f"p4sf{side}")
        dstep(["tm32b"], ["tm32b"],
               lambda e: e.tensor_scalar(tm32b[:, :], tm32b[:, :], 0.0, float(NSLOT - 1),
                                         OP.max, OP.min),
               tag=f"p4sc{side}")
        dstep(["tm32b"], [f"sl{side}i"],
               lambda e, sli=sli: e.tensor_copy(sli[:, :], tm32b[:, :]), tag=f"p4si{side}")

    if "dbg" in DEBUG_TAPS:
        b.step("sync", ["ltall"], ["dbglt"], lambda e: e.dma_start(out=dbg_lt[:, :, :], in_=lt_all[:, :, :]), q="dsy", tag="dbglt")
        b.step("sync", ["e1a"], ["dbge1"], lambda e: e.dma_start(out=dbg_e1[:, :], in_=e1a[:, :]), q="dsy", tag="dbge1")
        b.step("sync", ["r1a"], ["dbgr1"], lambda e: e.dma_start(out=dbg_r1[:, :], in_=r1a[:, :]), q="dsy", tag="dbgr1")
        b.step("sync", ["sl1i"], ["dbgsl"], lambda e: e.dma_start(out=dbg_sl1[:, :], in_=sl1i[:, :]), q="dsy", tag="dbgsl1")
        b.step("sync", ["sl2i"], ["dbgsl"], lambda e: e.dma_start(out=dbg_sl2[:, :], in_=sl2i[:, :]), q="dsy", tag="dbgsl2")
        b.step("sync", ["gts1"], ["dbgg"], lambda e: e.dma_start(out=dbg_g1[:, :], in_=gts1[:, :]), q="dsy", tag="dbgg1")
        b.step("sync", ["gts2"], ["dbgg"], lambda e: e.dma_start(out=dbg_g2[:, :], in_=gts2[:, :]), q="dsy", tag="dbgg2")

    for i in range(N3T):
        t0 = i * W3T
        p = i % 2
        b.step("sync", ["xfd"], [f"xft.{p}"],
               lambda e, t0=t0, p=p: e.dma_start(out=xft[p][:, :], in_=xf_d[t0:t0 + W3T, :]),
               q="dsy", tag=f"p4xl{i}")
        b.step("pool", [f"xft.{p}", "sl1i"], ["bufs"],
               lambda e, i=i, p=p: e.indirect_dma_start(
                   out=buf_d[:, :], out_offset=IndirectOffsetOnAxis(ap=sl1i[:, i:i + 1], axis=0),
                   in_=xft[p][:, :], in_offset=None),
               q="dgp", tag=f"p4s1{i}")
        b.step("pool", [f"xft.{p}", "sl2i"], ["bufs"],
               lambda e, i=i, p=p: e.indirect_dma_start(
                   out=buf_d[:, :], out_offset=IndirectOffsetOnAxis(ap=sl2i[:, i:i + 1], axis=0),
                   in_=xft[p][:, :], in_offset=None),
               q="dgp", tag=f"p4s2{i}")

    b.barrier()
    p4.close()

    p5 = contextlib.ExitStack()
    SLOTGRP = 512
    NGRP = SLOTCAP // SLOTGRP
    w2e_sb = p5.enter_context(nc.sbuf_tensor("w2e_sb", [128, IC, C], BF16))
    w1sb = [p5.enter_context(nc.sbuf_tensor(f"w1sb{j}", [128, CC, 512], BF16)) for j in range(2)]
    gtb = [p5.enter_context(nc.sbuf_tensor(f"gtb{j}", [128, 4, C], BF16)) for j in range(2)]
    xfm = [p5.enter_context(nc.sbuf_tensor(f"xfm{j}", [128, CC, SLOTGRP], BF16)) for j in range(2)]
    h_sb = p5.enter_context(nc.sbuf_tensor("h_sb", [128, IC, SLOTGRP], BF16))
    om = p5.enter_context(nc.sbuf_tensor("om", [128, CC, SLOTGRP], F32))
    postg = [p5.enter_context(nc.sbuf_tensor(f"postg{j}", [128, C], F32)) for j in range(2)]
    ptin = p5.enter_context(nc.psum_tensor("ptin", [128, 512], F32))
    pm1 = [p5.enter_context(nc.psum_tensor(f"pm1{j}", [128, 512], F32)) for j in range(2)]
    pm2 = [p5.enter_context(nc.psum_tensor(f"pm2{j}", [128, 512], F32)) for j in range(2)]
    pto = p5.enter_context(nc.psum_tensor("pto", [128, 1024], F32))

    for ex in range(NEXP):
        for fg in range(CC):
            def _w2(e, ex=ex, fg=fg):
                return e.dma_start(
                    out=w2e_sb[:, fg * 4:(fg + 1) * 4, :],
                    in_=w2[ex, fg * 512:(fg + 1) * 512, :].rearrange("(fc p) c -> p fc c", p=128))
            b.step("pool", ["w2e"], ["w2e"], _w2, q="dgp", tag=f"p5w2{ex}_{fg}")
        for g in range(NGRP):
            sbase = ex * SLOTCAP + g * SLOTGRP
            gp = g % 2
            b.step("sync", ["bufs"], [f"gtb.{gp}"],
                   lambda e, sbase=sbase, gp=gp: e.dma_start(
                       out=gtb[gp][:, :, :],
                       in_=buf_d[sbase:sbase + SLOTGRP, :].rearrange("(a b) c -> b a c", b=128)),
                   q="dsy", tag=f"p5g{ex}_{g}")
            for j in range(SLOTGRP // 128):
                def _tin(e, gp=gp, j=j):
                    r = None
                    pv = ptin[:, :].bitcast(BF16)
                    for cc in range(CC):
                        r = e.transpose(pv[:, cc * 128:(cc + 1) * 128],
                                        gtb[gp][:, j, cc * 128:(cc + 1) * 128], identbf[:, :])
                    return r
                b.step("pe", [f"gtb.{gp}", "identbf"], ["ptin"], _tin, tag=f"p5t{ex}_{g}_{j}")
                b.step("dve", ["ptin"], [f"xfm.{gp}"],
                       lambda e, gp=gp, j=j: e.tensor_copy(
                           xfm[gp][:, :, j * 128:(j + 1) * 128],
                           ptin[:, :].bitcast(BF16)[:, 0:C].rearrange("p (cc t) -> p cc t", cc=CC)),
                       tag=f"p5te{ex}_{g}_{j}")
            for fg in range(CC):
                wp = fg % 2
                def _w1(e, ex=ex, fg=fg, wp=wp):
                    return e.dma_start(
                        out=w1sb[wp][:, :, :],
                        in_=w1[ex, :, fg * 512:(fg + 1) * 512].rearrange("(kc p) f -> p kc f", p=128))
                b.step("pool", [f"w1b.{wp}"], [f"w1b.{wp}"], _w1, q="dgp", tag=f"p5w1{ex}_{g}_{fg}")
                for fl in range(4):
                    fc = fg * 4 + fl
                    def _m1(e, fl=fl, wp=wp, fc=fc, gp=gp):
                        r = None
                        for kc in range(CC):
                            r = e.matmul(pm1[fc % 2][:, :],
                                         w1sb[wp][:, kc, fl * 128:(fl + 1) * 128],
                                         xfm[gp][:, kc, :],
                                         start=(kc == 0), stop=(kc == CC - 1))
                        return r
                    b.step("pe", [f"w1b.{wp}", f"xfm.{gp}"], [f"pm1.{fc % 2}"], _m1,
                           tag=f"p5m1{ex}_{g}_{fc}")
                    b.step("act", [f"pm1.{fc % 2}"], ["h_sb"],
                           lambda e, fc=fc: e.activation(out=h_sb[:, fc, :], in_=pm1[fc % 2][:, :],
                                                         func=AF.Silu),
                           tag=f"p5si{ex}_{g}_{fc}")
            for occ in range(CC):
                pp2 = occ % 2
                def _m2(e, occ=occ, pp2=pp2):
                    r = None
                    for fc in range(IC):
                        r = e.matmul(pm2[pp2][:, :],
                                     w2e_sb[:, fc, occ * 128:(occ + 1) * 128],
                                     h_sb[:, fc, :],
                                     start=(fc == 0), stop=(fc == IC - 1))
                    return r
                b.step("pe", ["w2e", "h_sb"], [f"pm2.{pp2}"], _m2, tag=f"p5m2{ex}_{g}_{occ}")
                b.step("dve", [f"pm2.{pp2}"], ["om"],
                       lambda e, occ=occ, pp2=pp2: e.tensor_copy(om[:, occ, :], pm2[pp2][:, :]),
                       tag=f"p5oe{ex}_{g}_{occ}")
            for j in range(SLOTGRP // 128):
                jp = j % 2
                def _tout(e, j=j):
                    r = None
                    for cc in range(CC):
                        r = e.transpose(pto[:, cc * 128:(cc + 1) * 128],
                                        om[:, cc, j * 128:(j + 1) * 128], ident)
                    return r
                b.step("pe", ["om", "consts"], ["pto"], _tout, tag=f"p5to{ex}_{g}_{j}")
                b.step("dve", ["pto"], [f"postg.{jp}"],
                       lambda e, jp=jp: e.tensor_copy(postg[jp][:, :], pto[:, 0:C]),
                       tag=f"p5toe{ex}_{g}_{j}")
                b.step("sync", [f"postg.{jp}"], ["pod"],
                       lambda e, sbase=sbase, j=j, jp=jp: e.dma_start(
                           out=po_d[sbase + j * 128:sbase + (j + 1) * 128, :], in_=postg[jp][:, :]),
                       q="dsy", tag=f"p5ps{ex}_{g}_{j}")

        b.barrier()
    p5.close()

    p6 = contextlib.ExitStack()
    gb1 = [p6.enter_context(nc.sbuf_tensor(f"gb1{j}", [128, C], F32)) for j in range(2)]
    gb2 = [p6.enter_context(nc.sbuf_tensor(f"gb2{j}", [128, C], F32)) for j in range(2)]
    sld = [p6.enter_context(nc.sbuf_tensor(f"sld{j}", [128, C], F32)) for j in range(2)]
    ta6 = [p6.enter_context(nc.sbuf_tensor(f"ta6{j}", [128, C], F32)) for j in range(2)]
    ot6 = [p6.enter_context(nc.sbuf_tensor(f"ot6{j}", [128, C], F32)) for j in range(2)]

    for i in range(N3T):
        t0 = i * W3T
        p = i % 2
        b.step("pool", ["pod", "sl1i"], [f"gb1.{p}"],
               lambda e, i=i, p=p: e.indirect_dma_start(
                   out=gb1[p][:, :], out_offset=None, in_=po_d[:, :],
                   in_offset=IndirectOffsetOnAxis(ap=sl1i[:, i:i + 1], axis=0)),
               q="dgp", tag=f"p6g1{i}")
        b.step("pool", ["pod", "sl2i"], [f"gb2.{p}"],
               lambda e, i=i, p=p: e.indirect_dma_start(
                   out=gb2[p][:, :], out_offset=None, in_=po_d[:, :],
                   in_offset=IndirectOffsetOnAxis(ap=sl2i[:, i:i + 1], axis=0)),
               q="dgp", tag=f"p6g2{i}")
        b.step("sync", ["sd"], [f"sld.{p}"],
               lambda e, t0=t0, p=p: e.dma_start(out=sld[p][:, :], in_=s_d[t0:t0 + W3T, :]),
               q="dsy", tag=f"p6ls{i}")
        b.step("act", [f"gb1.{p}", "gts1"], [f"ta6.{p}"],
               lambda e, i=i, p=p: e.activation(out=ta6[p][:, :], in_=gb1[p][:, :],
                                                func=AF.Copy, scale=gts1[:, i:i + 1]),
               tag=f"p6a1{i}")
        b.step("dve", [f"ta6.{p}", f"sld.{p}"], [f"ot6.{p}"],
               lambda e, p=p: e.tensor_tensor(ot6[p][:, :], sld[p][:, :], ta6[p][:, :], OP.add),
               tag=f"p6x1{i}")
        b.step("act", [f"gb2.{p}", "gts2"], [f"ta6.{p}"],
               lambda e, i=i, p=p: e.activation(out=ta6[p][:, :], in_=gb2[p][:, :],
                                                func=AF.Copy, scale=gts2[:, i:i + 1]),
               tag=f"p6a2{i}")
        b.step("dve", [f"ta6.{p}", f"ot6.{p}"], [f"sld.{p}"],
               lambda e, p=p: e.tensor_tensor(sld[p][:, :], ot6[p][:, :], ta6[p][:, :], OP.add),
               tag=f"p6x2{i}")
        b.step("sync", [f"sld.{p}"], ["outd"],
               lambda e, t0=t0, p=p: e.dma_start(out=out_d[t0:t0 + W3T, :], in_=sld[p][:, :]),
               q="dsy", tag=f"p6o{i}")

    b.barrier()
    return nc, b


def build_nc():
    nc, b = build_program()
    with contextlib.ExitStack() as st:
        sems = {e: st.enter_context(nc.semaphore(f"sem_{e}")) for e in ENGN}
        dma_sems = {q: st.enter_context(nc.semaphore(f"dma_{q}"))
                    for q in ("dsy", "dsc", "dgp")}
        bar = st.enter_context(nc.semaphore("barr"))
        block = st.enter_context(nc.Block())
        b.lower(block, None, sems, dma_sems, bar)
    b.ctx.close()
    return nc



_NC_CACHE = []


def _host_consts():
    import ml_dtypes
    cst = np.zeros((128, 528), np.float32)
    cst[:, 0:128] = np.eye(128, dtype=np.float32)
    cst[:, 128:256] = np.triu(np.ones((128, 128), np.float32), 1)
    cst[:, 256:264] = np.tile(np.arange(8, dtype=np.float32), (128, 1))
    vals = [1.0, -1.0, -0.5, 1e-11, 1e-5, 1e-6, 1e-8, 1.0 + 1e-6]
    cst[:, 264:272] = np.asarray(vals, np.float32)[None, :]
    cst[:, 272:400] = 1.0 / 1024.0
    cst[:, 400:528] = 1.0
    identbf = np.eye(128).astype(ml_dtypes.bfloat16)
    return cst, identbf


def kernel(**inputs):
    f32 = lambda k: np.ascontiguousarray(np.asarray(inputs[k], dtype=np.float32))
    x = f32("L0_attn_x")
    wkey, wval, wrec = f32("L0_attn_W_key"), f32("L0_attn_W_value"), f32("L0_attn_W_recept")
    wout_, wrec2 = f32("L0_attn_W_out"), f32("L0_attn_W_recept2")
    wkey2, wval2 = f32("L0_attn_W_key2"), f32("L0_attn_W_value2")
    rw_ = f32("L0_ffn_router_w")
    w1_, w2_ = f32("L0_ffn_w1"), f32("L0_ffn_w2")

    def col(k):
        v = f32(k)
        return v.reshape(CC, 128).T
    vecs = np.stack([
        col("L0_attn_time_decay"), col("L0_attn_time_first"),
        col("L0_attn_time_mix_k"), col("L0_attn_time_mix_v"), col("L0_attn_time_mix_r"),
        col("L0_attn_time_mix_k2"), col("L0_attn_time_mix_r2"),
        col("L0_attn_ln1_w"), col("L0_attn_ln2_w"),
    ], axis=-1)
    vecs = np.ascontiguousarray(vecs)
    cst, identbf = _host_consts()
    rwt = np.ascontiguousarray(rw_.T)

    if not _NC_CACHE:
        _NC_CACHE.append(build_nc())
    nc = _NC_CACHE[0]

    in_maps = []
    for c in range(NCORES):
        xc = np.ascontiguousarray(
            x[c * NB:(c + 1) * NB].reshape(NTOK, C).T)
        ltc = (np.arange(NEXP) < c).astype(np.float32).reshape(NEXP, 1)
        in_maps.append({
            "x0f": xc, "wkey": wkey, "wval": wval, "wrec": wrec,
            "wout": wout_, "wrec2": wrec2, "wkey2": wkey2, "wval2": wval2,
            "rwt": rwt, "w1": w1_, "w2": w2_, "vecs_i": vecs,
            "consts_i": cst, "identbf_i": identbf, "ltcore": ltc,
        })
    res = run_bass_kernel_spmd(nc, in_maps, core_ids=list(range(NCORES)))
    out = np.empty((B_FULL, T, C), np.float32)
    for c, r in enumerate(res.results):
        out[c * NB:(c + 1) * NB] = np.asarray(r["outp"], np.float32).reshape(NB, T, C)
    return out


# version 13
# speedup vs baseline: 1.0320x; 1.0320x over previous
import contextlib
import numpy as np
import concourse.bass as bass
import concourse.mybir as mybir
from concourse.bass import IndirectOffsetOnAxis
from concourse.bass_utils import run_bass_kernel_spmd

F32 = mybir.dt.float32
BF16 = mybir.dt.bfloat16
I32 = mybir.dt.int32
AF = mybir.ActivationFunctionType
OP = mybir.AluOpType

NCORES = 8
B_FULL, T, C = 16, 2048, 1024
NB = B_FULL // NCORES
NTOK = NB * T
CC = C // 128
I_FF = 4096
IC = I_FF // 128
NEXP, TOPK = 8, 2
CAPACITY = int(1.25 * B_FULL * T * TOPK / NEXP) + 1
SLOTCAP = 1536
NSLOT = NEXP * SLOTCAP
W1T = 256
N1T = NTOK // W1T
W3T = 128
N3T = NTOK // W3T
EXP_CLIP_LO = float(np.exp(np.float32(-20.0), dtype=np.float32))
EXP_CLIP_HI = float(np.exp(np.float32(20.0), dtype=np.float32))

V_TD, V_U, V_TMK, V_TMV, V_TMR, V_TMK2, V_TMR2, V_LN1, V_LN2 = range(9)
NVEC = 9
D_EW, D_1MK, D_1MV, D_1MR, D_1MK2, D_1MR2 = range(6)
NDV = 6

DEBUG_TAPS = ()
USE_COLLECTIVE = False

ENGN = ("sync", "pool", "pe", "act", "dve")


class Step:
    __slots__ = ("eng", "reads", "writes", "emit", "q", "tag", "needs_inc", "sid", "deps")

    def __init__(self, eng, reads, writes, emit, q=None, tag=""):
        self.eng = eng
        self.reads = reads
        self.writes = writes
        self.emit = emit
        self.q = q
        self.tag = tag
        self.needs_inc = False
        self.sid = -1
        self.deps = []


class Builder:

    def __init__(self, nc):
        self.nc = nc
        self.steps = []
        self.last_w = {}
        self.readers = {}
        self.barrier_count = 0
        self.ctx = contextlib.ExitStack()

    def step(self, eng, reads, writes, emit, q=None, tag=""):
        s = Step(eng, list(reads), list(writes), emit, q, tag)
        s.sid = len(self.steps)
        self.steps.append(s)
        for k in s.reads:
            self.readers.setdefault(k, []).append(s)
        for k in s.writes:
            self.last_w[k] = s
            self.readers[k] = []
        return s

    def barrier(self):
        s = Step("BARRIER", [], [], None, None, f"bar{self.barrier_count}")
        s.sid = len(self.steps)
        self.steps.append(s)
        self.barrier_count += 1
        self.last_w = {}
        self.readers = {}

    def _resolve(self):
        last_w, readers = {}, {}
        for s in self.steps:
            if s.eng == "BARRIER":
                last_w, readers = {}, {}
                s.deps = []
                continue
            deps = []
            for k in s.reads:
                w = last_w.get(k)
                if w is not None:
                    deps.append(w)
            for k in s.writes:
                for r in readers.get(k, ()):
                    deps.append(r)
                w = last_w.get(k)
                if w is not None:
                    deps.append(w)
            uniq = {}
            for d in deps:
                if d is s:
                    continue
                if d.q is None and d.eng == s.eng:
                    continue
                uniq[d.sid] = d
            s.deps = [uniq[i] for i in sorted(uniq)]
            for d in s.deps:
                d.needs_inc = True
            for k in s.reads:
                readers.setdefault(k, []).append(s)
            for k in s.writes:
                last_w[k] = s
                readers[k] = []
            if s.tag == "scan":
                for k in s.reads:
                    w = last_w.get(k)
                    if w is not None and w.q is None and w.eng == "dve" and w is not s:
                        covered = any(w in d.deps for d in s.deps)
                        if not covered:
                            raise AssertionError(
                                f"scan step reads {k} produced by same-engine dve "
                                f"step {w.tag} without a cross-engine relay")

    def lower(self, block, engines, sems, dma_sems, bar):
        self._resolve()
        eng_cnt = {e: 0 for e in ENGN}
        q_cnt = {q: 0 for q in dma_sems}
        step_val = {}
        for s in self.steps:
            if s.eng == "BARRIER":
                continue
            if s.q is not None:
                q_cnt[s.q] += 16
                step_val[s.sid] = ("q", s.q, q_cnt[s.q])
            elif s.needs_inc:
                eng_cnt[s.eng] += 1
                step_val[s.sid] = ("e", s.eng, eng_cnt[s.eng])

        highwater = {e: {} for e in ENGN}
        cur_q = {q: 0 for q in dma_sems}

        per_eng = {e: [] for e in ENGN}
        nbar = 0
        q_running = {q: 0 for q in dma_sems}
        e_running = {e: 0 for e in ENGN}
        for s in self.steps:
            if s.eng == "BARRIER":
                nbar += 1
                snap_q = dict(q_running)
                snap_e = dict(e_running)
                for e in ENGN:
                    per_eng[e].append(("bar", nbar, snap_q, snap_e))
                continue
            per_eng[s.eng].append(("step", s))
            if s.q is not None:
                q_running[s.q] += 16
            elif s.needs_inc:
                e_running[s.eng] += 1

        for e in ENGN:
            def _section(eng, e=e):
                hw = highwater[e]
                for item in per_eng[e]:
                    if item[0] == "bar":
                        _, k, snap_q, snap_e = item
                        for q, v in snap_q.items():
                            if v and hw.get(("q", q), 0) < v:
                                eng.wait_ge(dma_sems[q], v)
                                hw[("q", q)] = v
                        eng.nop().then_inc(bar, 1)
                        eng.wait_ge(bar, len(ENGN) * k)
                        for e2, v in snap_e.items():
                            if v:
                                hw[("e", e2)] = max(hw.get(("e", e2), 0), v)
                        for q, v in snap_q.items():
                            if v:
                                hw[("q", q)] = max(hw.get(("q", q), 0), v)
                        continue
                    st = item[1]
                    for d in st.deps:
                        sv = step_val.get(d.sid)
                        if sv is None:
                            continue
                        kind, name, val = sv
                        key = (kind, name)
                        if hw.get(key, 0) >= val:
                            continue
                        if kind == "e":
                            eng.wait_ge(sems[name], val)
                        else:
                            eng.wait_ge(dma_sems[name], val)
                        hw[key] = val
                    inst = st.emit(eng)
                    if st.q is not None:
                        inst.then_inc(dma_sems[st.q], 16)
                    elif st.needs_inc:
                        if inst is None:
                            raise AssertionError(f"step {st.tag} emitted nothing")
                        inst.then_inc(sems[e], 1)
            getattr(block, self._section_name(e))(_section)

    @staticmethod
    def _section_name(e):
        return {"sync": "sync", "pool": "gpsimd", "pe": "tensor",
                "act": "scalar", "dve": "vector"}[e]



def build_program():
    nc = bass.Bass("TRN2", target_bir_lowering=False)
    b = Builder(nc)
    ctx = b.ctx

    def din(name, shape, dt=F32):
        return nc.dram_tensor(name, list(shape), dt, kind="ExternalInput").ap()

    x0f = din("x0f", [C, NTOK])
    wkey = din("wkey", [C, C]); wval = din("wval", [C, C]); wrec = din("wrec", [C, C])
    wout = din("wout", [C, C]); wrec2 = din("wrec2", [C, C])
    wkey2 = din("wkey2", [C, I_FF]); wval2 = din("wval2", [I_FF, C])
    rwt = din("rwt", [C, NEXP])
    w1 = din("w1", [NEXP, C, I_FF]); w2 = din("w2", [NEXP, I_FF, C])
    vecs_in = din("vecs_i", [128, CC, NVEC])
    consts_in = din("consts_i", [128, 528])
    identbf_in = din("identbf_i", [128, 128], BF16)
    ltcore_in = din("ltcore", [NEXP, 1])

    out_d = nc.dram_tensor("outp", [NTOK, C], F32, kind="ExternalOutput").ap()

    def scratch(name, shape, dt=F32):
        kind = "ExternalOutput" if name in DEBUG_TAPS else "Internal"
        return nc.dram_tensor(name, list(shape), dt, kind=kind).ap()

    ks_d = scratch("ks", [128, CC, NTOK])
    vs_d = scratch("vs", [128, CC, NTOK])
    rs_d = scratch("rs", [128, CC, NTOK])
    wo_d = scratch("wo_s", [128, CC, NTOK])
    xf_d = scratch("xf_s", [NTOK, C], BF16)
    s_d = scratch("s_s", [NTOK, C])
    buf_d = scratch("buf_s", [NSLOT, C], BF16)
    po_d = scratch("po_s", [NSLOT, C])
    ccin_d = scratch("cc_in", [1, NEXP])
    if "dbg" in DEBUG_TAPS:
        dbg_xa = nc.dram_tensor("dbg_xa", [128, CC, NTOK], F32, kind="ExternalOutput").ap()
        dbg_r2 = nc.dram_tensor("dbg_r2", [128, CC, NTOK], F32, kind="ExternalOutput").ap()
        dbg_cm = nc.dram_tensor("dbg_cm", [128, CC, NTOK], F32, kind="ExternalOutput").ap()
        dbg_x1 = nc.dram_tensor("dbg_x1", [128, CC, NTOK], F32, kind="ExternalOutput").ap()
        dbg_k2 = nc.dram_tensor("dbg_k2", [128, IC, NTOK], F32, kind="ExternalOutput").ap()
        dbg_sl1 = nc.dram_tensor("dbg_sl1", [128, N3T], I32, kind="ExternalOutput").ap()
        dbg_sl2 = nc.dram_tensor("dbg_sl2", [128, N3T], I32, kind="ExternalOutput").ap()
        dbg_g1 = nc.dram_tensor("dbg_g1", [128, N3T], F32, kind="ExternalOutput").ap()
        dbg_g2 = nc.dram_tensor("dbg_g2", [128, N3T], F32, kind="ExternalOutput").ap()
        dbg_lt = nc.dram_tensor("dbg_lt", [128, N3T, NEXP], F32, kind="ExternalOutput").ap()
        dbg_e1 = nc.dram_tensor("dbg_e1", [128, N3T], F32, kind="ExternalOutput").ap()
        dbg_r1 = nc.dram_tensor("dbg_r1", [128, N3T], F32, kind="ExternalOutput").ap()
    ccout_d = nc.dram_tensor("cc_out", [NEXP, NEXP], F32, kind="Internal",
                             addr_space="Shared").ap()

    def sb(name, shape, dt=F32):
        return ctx.enter_context(nc.sbuf_tensor(name, list(shape), dt, side="right"))

    vec = sb("vec", [128, CC, NVEC])
    dvec = sb("dvec", [128, CC, 11])
    consts = sb("consts", [128, 528])
    identbf = sb("identbf", [128, 128], BF16)
    ltsb = sb("ltsb", [NEXP, 1])
    routersb = sb("routersb", [128, CC, NEXP], BF16)
    lt_all = sb("lt_all", [128, N3T, NEXP])
    e1a = sb("e1a", [128, N3T]); e2a = sb("e2a", [128, N3T])
    r1a = sb("r1a", [128, N3T]); r2a = sb("r2a", [128, N3T])
    g1a = sb("g1a", [128, N3T]); g2a = sb("g2a", [128, N3T])
    gts1 = sb("gts1", [128, N3T]); gts2 = sb("gts2", [128, N3T])
    sl1i = sb("sl1i", [128, N3T], I32); sl2i = sb("sl2i", [128, N3T], I32)

    ident = consts[:, 0:128]
    LT128 = consts[:, 128:256]
    iota8 = consts[:, 256:264]
    cstc = consts[:, 264:272]
    C_ONE, C_M1, C_MH = cstc[:, 0:1], cstc[:, 1:2], cstc[:, 2:3]
    C_E11, C_E5, C_E6, C_E8, C_1P = (cstc[:, 3:4], cstc[:, 4:5], cstc[:, 5:6],
                                     cstc[:, 6:7], cstc[:, 7:8])
    onesC = consts[:, 272:400]
    ones128 = consts[:, 400:528]

    DX_EW = 0
    DX_KA, DX_KB, DX_VA, DX_VB, DX_RA, DX_RB = 1, 2, 3, 4, 5, 6
    DX_K2A, DX_K2B, DX_R2A, DX_R2B = 7, 8, 9, 10

    def bc(ap2d, reps):
        return ap2d.unsqueeze(1).broadcast_to([128, reps, ap2d.shape[-1]])

    p02 = ctx.enter_context(contextlib.ExitStack())
    wo_sb = p02.enter_context(nc.sbuf_tensor("wo_sb", [128, CC, C], BF16))
    p01 = contextlib.ExitStack()
    zt = p01.enter_context(nc.sbuf_tensor("zt", [128, 4096], BF16))
    tmpv = p01.enter_context(nc.sbuf_tensor("tmpv", [128, CC], F32))

    b.step("sync", [], ["vec"], lambda e: e.dma_start(out=vec[:, :, :], in_=vecs_in[:, :, :]), q="dsy", tag="ld_vec")
    b.step("sync", [], ["consts"], lambda e: e.dma_start(out=consts[:, :], in_=consts_in[:, :]), q="dsy", tag="ld_consts")
    b.step("sync", [], ["identbf"], lambda e: e.dma_start(out=identbf[:, :], in_=identbf_in[:, :]), q="dsy", tag="ld_identbf")
    b.step("sync", [], ["ltsb"], lambda e: e.dma_start(out=ltsb[:, :], in_=ltcore_in[:, :]), q="dsy", tag="ld_ltsb")
    b.step("pool", [], ["routersb"], lambda e: e.dma_start(out=routersb[:, :, :], in_=rwt.rearrange("(cc p) e -> p cc e", p=128)), q="dgp", tag="ld_router")

    b.step("act", ["vec"], ["tmpv"], lambda e: e.activation(out=tmpv[:, :], in_=vec[:, :, V_TD], func=AF.Exp), tag="ew1")
    b.step("pool", ["tmpv"], ["tmpv"], lambda e: e.tensor_scalar(tmpv[:, :], tmpv[:, :], 20.0, None, OP.min), tag="ew2")
    b.step("act", ["tmpv", "consts"], ["dvec"], lambda e: e.activation(out=dvec[:, :, DX_EW], in_=tmpv[:, :], func=AF.Exp, scale=C_M1), tag="ew3")

    def mk_coeff(dst_a, dst_b, v_tm, v_ln, tagn):
        def _a(e, dst_a=dst_a, v_tm=v_tm, v_ln=v_ln):
            return e.tensor_tensor(dvec[:, :, dst_a], vec[:, :, v_tm], vec[:, :, v_ln], OP.mult)
        def _b(e, dst_a=dst_a, dst_b=dst_b, v_ln=v_ln):
            return e.tensor_tensor(dvec[:, :, dst_b], vec[:, :, v_ln], dvec[:, :, dst_a], OP.subtract)
        b.step("pool", ["vec"], ["dvec"], _a, tag=f"cf{tagn}a")
        b.step("pool", ["vec", "dvec"], ["dvec"], _b, tag=f"cf{tagn}b")

    mk_coeff(DX_KA, DX_KB, V_TMK, V_LN1, "k")
    mk_coeff(DX_VA, DX_VB, V_TMV, V_LN1, "v")
    mk_coeff(DX_RA, DX_RB, V_TMR, V_LN1, "r")
    mk_coeff(DX_K2A, DX_K2B, V_TMK2, V_LN2, "k2")
    mk_coeff(DX_R2A, DX_R2B, V_TMR2, V_LN2, "r2")

    b.step("pool", [], ["zt"], lambda e: e.memset(zt[:, :], 0.0), tag="zt0")
    for r0 in range(0, NSLOT, 512):
        def _z(e, r0=r0):
            return e.dma_start(
                out=buf_d[r0:r0 + 512, :].rearrange("(a b) c -> b a c", b=128),
                in_=zt[:, :].rearrange("p (a c) -> p a c", a=4))
        b.step("act", ["zt"], ["bufs"], _z, q="dsc", tag=f"zb{r0}")

    wk_sb = p01.enter_context(nc.sbuf_tensor("wk_sb", [128, CC, C], BF16))
    wv_sb = p01.enter_context(nc.sbuf_tensor("wv_sb", [128, CC, C], BF16))
    wr_sb = p01.enter_context(nc.sbuf_tensor("wr_sb", [128, CC, C], BF16))
    for nm, dst, src in (("k", wk_sb, wkey), ("v", wv_sb, wval),
                         ("r", wr_sb, wrec), ("wo", wo_sb, wout)):
        def _w(e, dst=dst, src=src):
            return e.dma_start(out=dst[:, :, :], in_=src.rearrange("(kc p) m -> p kc m", p=128))
        b.step("pool", [], [nm], _w, q="dgp", tag=f"ld{nm}")


    x0b = [p01.enter_context(nc.sbuf_tensor(f"x0b{j}", [128, CC, W1T], F32)) for j in range(2)]
    zb_ = [p01.enter_context(nc.sbuf_tensor(f"zb{j}", [128, CC, W1T], F32)) for j in range(2)]
    sqb_ = [p01.enter_context(nc.sbuf_tensor(f"sqb{j}", [128, CC, W1T], F32)) for j in range(2)]
    hb = [p01.enter_context(nc.sbuf_tensor(f"hb{j}", [128, CC, W1T + 1], F32)) for j in range(2)]
    shmb_ = [p01.enter_context(nc.sbuf_tensor(f"shmb{j}", [128, CC, W1T], F32)) for j in range(2)]
    xkb = [p01.enter_context(nc.sbuf_tensor(f"xkb{j}", [128, CC, W1T], BF16)) for j in range(2)]
    xvb = [p01.enter_context(nc.sbuf_tensor(f"xvb{j}", [128, CC, W1T], BF16)) for j in range(2)]
    xrb = [p01.enter_context(nc.sbuf_tensor(f"xrb{j}", [128, CC, W1T], BF16)) for j in range(2)]
    kst = [p01.enter_context(nc.sbuf_tensor("kst0", [128, CC, W1T], F32))] * 2
    vst = [p01.enter_context(nc.sbuf_tensor("vst0", [128, CC, W1T], F32))] * 2
    rst = [p01.enter_context(nc.sbuf_tensor("rst0", [128, CC, W1T], F32))] * 2
    bbca_ = [p01.enter_context(nc.sbuf_tensor(f"bbca{j}", [128, W1T], F32)) for j in range(2)]
    pmb = [p01.enter_context(nc.psum_tensor(f"pmb{j}", [128, 512], F32)) for j in range(2)]
    pvb = [p01.enter_context(nc.psum_tensor(f"pvb{j}", [128, 512], F32)) for j in range(2)]
    ppj = [p01.enter_context(nc.psum_tensor(f"ppj{j}", [128, 512], F32)) for j in range(4)]

    TPB1 = T // W1T

    for i in range(N1T):
        t0 = i * W1T
        p = i % 2

        b.step("sync", [], [f"x0.{p}"],
               lambda e, t0=t0, p=p: e.dma_start(
                   out=x0b[p][:, :, :],
                   in_=x0f[:, t0:t0 + W1T].rearrange("(cc q) t -> q cc t", q=128)),
               q="dsy", tag=f"p1ld{i}")

        def _mean(e, p=p):
            r = None
            for kc in range(CC):
                r = e.matmul(pmb[p][:, :W1T], onesC, x0b[p][:, kc, :],
                             start=(kc == 0), stop=(kc == CC - 1))
            return r
        b.step("pe", [f"x0.{p}", "consts"], [f"pmb.{p}"], _mean, tag=f"p1mean{i}")

        b.step("dve", [f"x0.{p}", f"pmb.{p}"], [f"zb.{p}"],
               lambda e, p=p: e.tensor_tensor(zb_[p][:, :, :], x0b[p][:, :, :],
                                              bc(pmb[p][:, :W1T], CC), OP.subtract),
               tag=f"p1z{i}")
        b.step("act", [f"zb.{p}"], [f"sqb.{p}"],
               lambda e, p=p: e.activation(out=sqb_[p][:, :, :], in_=zb_[p][:, :, :], func=AF.Square),
               tag=f"p1sq{i}")

        def _var(e, p=p):
            r = None
            for kc in range(CC):
                r = e.matmul(pvb[p][:, :W1T], onesC, sqb_[p][:, kc, :],
                             start=(kc == 0), stop=(kc == CC - 1))
            return r
        b.step("pe", [f"sqb.{p}", "consts"], [f"pvb.{p}"], _var, tag=f"p1var{i}")

        def _beta(e, p=p):
            e.activation(out=bbca_[p][:, :], in_=pvb[p][:, :W1T], func=AF.Ln,
                         scale=C_1P, bias=C_E11)
            return e.activation(out=bbca_[p][:, :], in_=bbca_[p][:, :], func=AF.Exp, scale=C_MH)
        b.step("act", [f"pvb.{p}", "consts"], [f"bbca.{p}"], _beta, tag=f"p1beta{i}")

        b.step("dve", [f"zb.{p}", f"bbca.{p}"], [f"hb.{p}"],
               lambda e, p=p: e.tensor_tensor(hb[p][:, :, 1:W1T + 1], zb_[p][:, :, :],
                                              bc(bbca_[p][:, :], CC), OP.mult),
               tag=f"p1h{i}")
        if i % TPB1 == 0:
            b.step("dve", [], [f"hbc.{p}"],
                   lambda e, p=p: e.memset(hb[p][:, :, 0:1], 0.0), tag=f"p1bz{i}")
        else:
            b.step("dve", [f"hb.{1 - p}"], [f"hbc.{p}"],
                   lambda e, p=p: e.tensor_copy(hb[p][:, :, 0:1], hb[1 - p][:, :, W1T:W1T + 1]),
                   tag=f"p1bc{i}")

        for nm, xa, xbuf, da, db in (("k", xkb, None, DX_KA, DX_KB),
                                     ("v", xvb, None, DX_VA, DX_VB),
                                     ("r", xrb, None, DX_RA, DX_RB)):
            dst = {"k": xkb, "v": xvb, "r": xrb}[nm][p]
            def _shm(e, p=p, db=db):
                r = None
                for cc in range(CC):
                    r = e.tensor_scalar(shmb_[p][:, cc, :], hb[p][:, cc, 0:W1T],
                                        dvec[:, cc, db:db + 1], None, OP.mult)
                return r
            b.step("dve", [f"hb.{p}", f"hbc.{p}", "dvec"], [f"shmb.{p}"], _shm, tag=f"p1sh{nm}{i}")
            def _mix(e, p=p, da=da, dst=dst):
                r = None
                for cc in range(CC):
                    r = e.scalar_tensor_tensor(dst[:, cc, :], hb[p][:, cc, 1:W1T + 1],
                                               dvec[:, cc, da:da + 1], shmb_[p][:, cc, :],
                                               OP.mult, OP.add)
                return r
            b.step("dve", [f"hb.{p}", "dvec", f"shmb.{p}"], [f"x{nm}.{p}"], _mix, tag=f"p1mx{nm}{i}")

        for jn, (nm, wsb, xsb) in enumerate((("k", wk_sb, xkb), ("v", wv_sb, xvb),
                                             ("r", wr_sb, xrb))):
            for occ in range(CC):
                bank = (i * 24 + jn * 8 + occ) % 4
                def _mm(e, p=p, occ=occ, wsb=wsb, xsb=xsb, bank=bank):
                    r = None
                    for kc in range(CC):
                        r = e.matmul(ppj[bank][:, :W1T], wsb[:, kc, occ * 128:(occ + 1) * 128],
                                     xsb[p][:, kc, :], start=(kc == 0), stop=(kc == CC - 1))
                    return r
                b.step("pe", [f"x{nm}.{p}", nm], [f"ppj.{bank}"], _mm, tag=f"p1p{nm}{occ}_{i}")
                if nm == "k":
                    b.step("dve", [f"ppj.{bank}"], ["kst"],
                           lambda e, p=p, occ=occ, bank=bank: e.tensor_copy(kst[p][:, occ, :], ppj[bank][:, :W1T]),
                           tag=f"p1ek{occ}_{i}")
                elif nm == "v":
                    b.step("dve", [f"ppj.{bank}"], ["vst"],
                           lambda e, p=p, occ=occ, bank=bank: e.tensor_copy(vst[p][:, occ, :], ppj[bank][:, :W1T]),
                           tag=f"p1ev{occ}_{i}")
                else:
                    b.step("act", [f"ppj.{bank}", "consts"], ["rst"],
                           lambda e, p=p, occ=occ, bank=bank: e.activation(
                               out=rst[p][:, occ, :], in_=ppj[bank][:, :W1T],
                               func=AF.Exp, scale=C_M1),
                           tag=f"p1er{occ}_{i}")
        def _rsig(e, p=p):
            e.activation(out=rst[p][:, :, :], in_=rst[p][:, :, :], func=AF.Ln, bias=C_ONE)
            return e.activation(out=rst[p][:, :, :], in_=rst[p][:, :, :], func=AF.Exp, scale=C_M1)
        b.step("act", ["rst", "consts"], ["rst"], _rsig, tag=f"p1rs{i}")

        b.step("sync", ["kst"], ["ksd"],
               lambda e, t0=t0, p=p: e.dma_start(out=ks_d[:, :, t0:t0 + W1T], in_=kst[p][:, :, :]),
               q="dsy", tag=f"p1sk{i}")
        b.step("sync", ["vst"], ["vsd"],
               lambda e, t0=t0, p=p: e.dma_start(out=vs_d[:, :, t0:t0 + W1T], in_=vst[p][:, :, :]),
               q="dsy", tag=f"p1sv{i}")
        b.step("sync", ["rst"], ["rsd"],
               lambda e, t0=t0, p=p: e.dma_start(out=rs_d[:, :, t0:t0 + W1T], in_=rst[p][:, :, :]),
               q="dsy", tag=f"p1sr{i}")

    b.barrier()
    p01.close()

    p2 = p02
    kt = [p2.enter_context(nc.sbuf_tensor(f"kt{j}", [128, T], F32)) for j in range(2)]
    vt = [p2.enter_context(nc.sbuf_tensor(f"vt{j}", [128, T], F32)) for j in range(2)]
    rt = [p2.enter_context(nc.sbuf_tensor(f"rt{j}", [128, T], F32)) for j in range(2)]
    e1t = p2.enter_context(nc.sbuf_tensor("e1t", [128, T], F32))
    e2t = p2.enter_context(nc.sbuf_tensor("e2t", [128, T], F32))
    htt = p2.enter_context(nc.sbuf_tensor("htt", [128, T], F32))
    gtt = p2.enter_context(nc.sbuf_tensor("gtt", [128, T], F32))
    att = p2.enter_context(nc.sbuf_tensor("att", [128, T], F32))
    btt = p2.enter_context(nc.sbuf_tensor("btt", [128, T], F32))
    invt = p2.enter_context(nc.sbuf_tensor("invt", [128, T], F32))
    wkvt = p2.enter_context(nc.sbuf_tensor("wkvt", [128, T], F32))
    ewbt = p2.enter_context(nc.sbuf_tensor("ewbt", [128, T], F32))
    rwsb = p2.enter_context(nc.sbuf_tensor("rwsb", [128, CC, T], BF16))
    wost = [p2.enter_context(nc.sbuf_tensor(f"wost{j}", [128, 512], F32)) for j in range(2)]
    pw2 = [p2.enter_context(nc.psum_tensor(f"pw2{j}", [128, 512], F32)) for j in range(2)]

    for bb in range(NB):
        bT = bb * T
        for cc in range(CC):
            u = bb * CC + cc
            p = u % 2
            b.step("sync", [], [f"kt.{p}"],
                   lambda e, bT=bT, cc=cc, p=p: e.dma_start(out=kt[p][:, :], in_=ks_d[:, cc, bT:bT + T]),
                   q="dsy", tag=f"p2lk{u}")
            b.step("sync", [], [f"vt.{p}"],
                   lambda e, bT=bT, cc=cc, p=p: e.dma_start(out=vt[p][:, :], in_=vs_d[:, cc, bT:bT + T]),
                   q="dsy", tag=f"p2lv{u}")
            b.step("sync", [], [f"rt.{p}"],
                   lambda e, bT=bT, cc=cc, p=p: e.dma_start(out=rt[p][:, :], in_=rs_d[:, cc, bT:bT + T]),
                   q="dsy", tag=f"p2lr{u}")

            b.step("act", [f"kt.{p}", "vec"], ["e1t"],
                   lambda e, cc=cc, p=p: e.activation(out=e1t[:, :], in_=kt[p][:, :],
                                                      func=AF.Exp, bias=vec[:, cc, V_U:V_U + 1]),
                   tag=f"p2e1{u}")
            b.step("dve", ["e1t"], ["e1t"],
                   lambda e: e.tensor_scalar(e1t[:, :], e1t[:, :], EXP_CLIP_LO, EXP_CLIP_HI, OP.max, OP.min),
                   tag=f"p2c1{u}")
            b.step("act", [f"kt.{p}"], ["e2t"],
                   lambda e, p=p: e.activation(out=e2t[:, :], in_=kt[p][:, :], func=AF.Exp),
                   tag=f"p2e2{u}")
            b.step("dve", ["e2t"], ["e2t"],
                   lambda e: e.tensor_scalar(e2t[:, :], e2t[:, :], EXP_CLIP_LO, EXP_CLIP_HI, OP.max, OP.min),
                   tag=f"p2c2{u}")
            b.step("dve", ["e1t", f"vt.{p}"], ["htt"],
                   lambda e, p=p: e.tensor_tensor(htt[:, :], e1t[:, :], vt[p][:, :], OP.mult),
                   tag=f"p2h{u}")
            b.step("dve", ["e2t", f"vt.{p}"], ["gtt"],
                   lambda e, p=p: e.tensor_tensor(gtt[:, :], e2t[:, :], vt[p][:, :], OP.mult),
                   tag=f"p2g{u}")
            b.step("pool", ["gtt", "e2t", "ewbt"], [f"p2rl{u}"], lambda e: e.nop(), tag=f"p2rl{u}")
            b.step("pool", [f"kt.{p}", "dvec"], ["ewbt"],
                   lambda e, cc=cc, p=p: e.tensor_scalar(ewbt[:, :], kt[p][:, :], 0.0,
                                                         dvec[:, cc, DX_EW:DX_EW + 1], OP.mult, OP.add),
                   tag=f"p2ew{u}")
            b.step("dve", ["ewbt", "gtt", f"p2rl{u}"], ["att"],
                   lambda e: e.tensor_tensor_scan(att[:, :], ewbt[:, :], gtt[:, :], 0.0, OP.mult, OP.add),
                   tag="scan")
            b.step("dve", ["ewbt", "e2t", f"p2rl{u}"], ["btt"],
                   lambda e: e.tensor_tensor_scan(btt[:, :], ewbt[:, :], e2t[:, :], 0.0, OP.mult, OP.add),
                   tag="scan")
            b.step("pool", ["att", "btt"], [f"p2rs{u}"], lambda e: e.nop(), tag=f"p2rs{u}")
            def _num(e):
                e.tensor_copy(gtt[:, 0:1], htt[:, 0:1])
                return e.tensor_tensor(gtt[:, 1:T], att[:, 0:T - 1], htt[:, 1:T], OP.add)
            b.step("dve", ["att", "htt", f"p2rs{u}"], ["gtt"], _num, tag=f"p2n{u}")
            def _den(e):
                e.tensor_copy(e2t[:, 0:1], e1t[:, 0:1])
                return e.tensor_tensor(e2t[:, 1:T], btt[:, 0:T - 1], e1t[:, 1:T], OP.add)
            b.step("dve", ["btt", "e1t", f"p2rs{u}"], ["e2t"], _den, tag=f"p2d{u}")
            def _inv(e):
                e.activation(out=invt[:, :], in_=e2t[:, :], func=AF.Ln, bias=C_E8)
                return e.activation(out=invt[:, :], in_=invt[:, :], func=AF.Exp, scale=C_M1)
            b.step("act", ["e2t", "consts"], ["invt"], _inv, tag=f"p2i{u}")
            b.step("dve", ["gtt", "invt"], ["wkvt"],
                   lambda e: e.tensor_tensor(wkvt[:, :], gtt[:, :], invt[:, :], OP.mult),
                   tag=f"p2w{u}")
            b.step("dve", ["wkvt", f"rt.{p}"], ["rwsb"],
                   lambda e, cc=cc, p=p: e.tensor_tensor(rwsb[:, cc, :], wkvt[:, :], rt[p][:, :], OP.mult),
                   tag=f"p2rw{u}")

        for jt in range(T // 512):
            for occ in range(CC):
                pk = (jt * CC + occ) % 2
                def _wo(e, jt=jt, occ=occ, pk=pk):
                    r = None
                    for kc in range(CC):
                        r = e.matmul(pw2[pk][:, :], wo_sb[:, kc, occ * 128:(occ + 1) * 128],
                                     rwsb[:, kc, jt * 512:(jt + 1) * 512],
                                     start=(kc == 0), stop=(kc == CC - 1))
                    return r
                b.step("pe", ["rwsb", "wo"], [f"pw2.{pk}"], _wo, tag=f"p2wo{bb}_{jt}_{occ}")
                b.step("dve", [f"pw2.{pk}"], [f"wost.{pk}"],
                       lambda e, pk=pk: e.tensor_copy(wost[pk][:, :], pw2[pk][:, :]),
                       tag=f"p2we{bb}_{jt}_{occ}")
                b.step("sync", [f"wost.{pk}"], ["wod"],
                       lambda e, bT=bT, jt=jt, occ=occ, pk=pk: e.dma_start(
                           out=wo_d[:, occ, bT + jt * 512:bT + (jt + 1) * 512], in_=wost[pk][:, :]),
                       q="dsy", tag=f"p2ws{bb}_{jt}_{occ}")

    b.barrier()
    p2.close()

    xk2_d = scratch("xk2_s", [128, CC, NTOK], BF16)
    r2_d = scratch("r2_s", [128, CC, NTOK])
    y_d = scratch("y_s", [128, CC, NTOK])

    W3A = 512
    N3A = NTOK // W3A
    p3 = contextlib.ExitStack()
    wr2_sb = p3.enter_context(nc.sbuf_tensor("wr2_sb", [128, CC, C], BF16))
    b.step("pool", [], ["wr2"], lambda e: e.dma_start(out=wr2_sb[:, :, :], in_=wrec2.rearrange("(kc p) m -> p kc m", p=128)), q="dgp", tag="ldwr2")

    wvo3 = p3.enter_context(nc.sbuf_tensor("wvo30", [128, CC, W3A], F32))
    x0c = p3.enter_context(nc.sbuf_tensor("x0c0", [128, CC, W3A], F32))
    t3a = p3.enter_context(nc.sbuf_tensor("t3a", [128, CC, W3A], F32))
    t3b = p3.enter_context(nc.sbuf_tensor("t3b", [128, CC, W3A], F32))
    xac = p3.enter_context(nc.sbuf_tensor("xac", [128, CC, W3A], F32))
    h2b = [p3.enter_context(nc.sbuf_tensor(f"h2b{j}", [128, CC, W3A + 1], F32)) for j in range(2)]
    shm2 = p3.enter_context(nc.sbuf_tensor("shm2", [128, CC, W3A], F32))
    xk2b = p3.enter_context(nc.sbuf_tensor("xk2b0", [128, CC, W3A], BF16))
    xr2b = p3.enter_context(nc.sbuf_tensor("xr2b0", [128, CC, W3A], BF16))
    r2c = p3.enter_context(nc.sbuf_tensor("r2c", [128, CC, W3A], F32))
    alc = p3.enter_context(nc.sbuf_tensor("alc", [128, W3A], F32))
    s3c = p3.enter_context(nc.sbuf_tensor("s3c", [128, W3A], F32))
    pstA = p3.enter_context(nc.psum_tensor("pstA", [128, 512], F32))
    pstB = p3.enter_context(nc.psum_tensor("pstB", [128, 512], F32))
    pr2 = [p3.enter_context(nc.psum_tensor(f"pr2{j}", [128, 512], F32)) for j in range(2)]

    TPBA = T // W3A

    for i in range(N3A):
        t0 = i * W3A
        p = i % 2

        b.step("sync", [], ["wvo3"],
               lambda e, t0=t0: e.dma_start(out=wvo3[:, :, :], in_=wo_d[:, :, t0:t0 + W3A]),
               q="dsy", tag=f"p3lw{i}")
        b.step("sync", [], ["x0c"],
               lambda e, t0=t0: e.dma_start(
                   out=x0c[:, :, :],
                   in_=x0f[:, t0:t0 + W3A].rearrange("(cc q) t -> q cc t", q=128)),
               q="dsy", tag=f"p3lx{i}")

        def _mean(e):
            r = None
            for kc in range(CC):
                r = e.matmul(pstA[:, :], onesC, x0c[:, kc, :],
                             start=(kc == 0), stop=(kc == CC - 1))
            return r
        b.step("pe", ["x0c", "consts"], ["pstA"], _mean, tag=f"p3mean{i}")
        b.step("dve", ["x0c", "pstA"], ["t3a"],
               lambda e: e.tensor_tensor(t3a[:, :, :], x0c[:, :, :],
                                         bc(pstA[:, :], CC), OP.subtract),
               tag=f"p3z{i}")
        b.step("act", ["t3a"], ["t3b"],
               lambda e: e.activation(out=t3b[:, :, :], in_=t3a[:, :, :], func=AF.Square),
               tag=f"p3sq{i}")
        def _var(e):
            r = None
            for kc in range(CC):
                r = e.matmul(pstB[:, :], onesC, t3b[:, kc, :],
                             start=(kc == 0), stop=(kc == CC - 1))
            return r
        b.step("pe", ["t3b", "consts"], ["pstB"], _var, tag=f"p3var{i}")
        def _alpha(e):
            e.activation(out=alc[:, :], in_=pstB[:, :], func=AF.Ln, bias=C_E5)
            return e.activation(out=alc[:, :], in_=alc[:, :], func=AF.Exp, scale=C_MH)
        b.step("act", ["pstB", "consts"], ["alc"], _alpha, tag=f"p3al{i}")
        b.step("dve", ["t3a", "alc"], ["t3b"],
               lambda e: e.tensor_tensor(t3b[:, :, :], t3a[:, :, :], bc(alc[:, :], CC), OP.mult),
               tag=f"p3ln0{i}")
        b.step("dve", ["t3b", "wvo3"], ["xac"],
               lambda e: e.tensor_tensor(xac[:, :, :], t3b[:, :, :], wvo3[:, :, :], OP.add),
               tag=f"p3xa{i}")
        b.step("dve", ["x0c", "xac"], ["t3b"],
               lambda e: e.tensor_tensor(t3b[:, :, :], x0c[:, :, :], xac[:, :, :], OP.add),
               tag=f"p3y{i}")
        b.step("sync", ["t3b"], ["yd"],
               lambda e, t0=t0: e.dma_start(out=y_d[:, :, t0:t0 + W3A], in_=t3b[:, :, :]),
               q="dsy", tag=f"p3sy{i}")
        b.step("act", ["xac"], ["t3a"],
               lambda e: e.activation(out=t3a[:, :, :], in_=xac[:, :, :], func=AF.Square),
               tag=f"p3sq2{i}")
        def _ms2(e):
            r = None
            for kc in range(CC):
                r = e.matmul(pstA[:, :], onesC, t3a[:, kc, :],
                             start=(kc == 0), stop=(kc == CC - 1))
            return r
        b.step("pe", ["t3a", "consts"], ["pstA"], _ms2, tag=f"p3ms2{i}")
        def _s3(e):
            e.activation(out=s3c[:, :], in_=pstA[:, :], func=AF.Ln, bias=C_E6)
            return e.activation(out=s3c[:, :], in_=s3c[:, :], func=AF.Exp, scale=C_MH)
        b.step("act", ["pstA", "consts"], ["s3c"], _s3, tag=f"p3s3{i}")
        b.step("dve", ["xac", "s3c"], [f"h2.{p}"],
               lambda e, p=p: e.tensor_tensor(h2b[p][:, :, 1:W3A + 1], xac[:, :, :],
                                              bc(s3c[:, :], CC), OP.mult),
               tag=f"p3h2{i}")
        if i % TPBA == 0:
            b.step("dve", [], [f"h2c.{p}"],
                   lambda e, p=p: e.memset(h2b[p][:, :, 0:1], 0.0), tag=f"p3bz{i}")
        else:
            b.step("dve", [f"h2.{1 - p}"], [f"h2c.{p}"],
                   lambda e, p=p: e.tensor_copy(h2b[p][:, :, 0:1], h2b[1 - p][:, :, W3A:W3A + 1]),
                   tag=f"p3bc{i}")

        for nm, da, db in (("k2", DX_K2A, DX_K2B), ("r2", DX_R2A, DX_R2B)):
            dst = {"k2": xk2b, "r2": xr2b}[nm]
            def _shm(e, p=p, db=db):
                r = None
                for cc in range(CC):
                    r = e.tensor_scalar(shm2[:, cc, :], h2b[p][:, cc, 0:W3A],
                                        dvec[:, cc, db:db + 1], None, OP.mult)
                return r
            b.step("dve", [f"h2.{p}", f"h2c.{p}", "dvec"], ["shm2"], _shm, tag=f"p3sh{nm}{i}")
            def _mix(e, p=p, da=da, dst=dst):
                r = None
                for cc in range(CC):
                    r = e.scalar_tensor_tensor(dst[:, cc, :], h2b[p][:, cc, 1:W3A + 1],
                                               dvec[:, cc, da:da + 1], shm2[:, cc, :],
                                               OP.mult, OP.add)
                return r
            b.step("dve", [f"h2.{p}", "dvec", "shm2"], [f"x{nm}b"], _mix, tag=f"p3mx{nm}{i}")
        b.step("sync", ["xk2b"], ["xk2d"],
               lambda e, t0=t0: e.dma_start(out=xk2_d[:, :, t0:t0 + W3A], in_=xk2b[:, :, :]),
               q="dsy", tag=f"p3sxk2{i}")

        for occ in range(CC):
            pk = occ % 2
            def _r2mm(e, occ=occ, pk=pk):
                r = None
                for kc in range(CC):
                    r = e.matmul(pr2[pk][:, :], wr2_sb[:, kc, occ * 128:(occ + 1) * 128],
                                 xr2b[:, kc, :], start=(kc == 0), stop=(kc == CC - 1))
                return r
            b.step("pe", ["xr2b", "wr2"], [f"pr2.{pk}"], _r2mm, tag=f"p3r2m{occ}_{i}")
            b.step("act", [f"pr2.{pk}", "consts"], ["r2c"],
                   lambda e, occ=occ, pk=pk: e.activation(
                       out=r2c[:, occ, :], in_=pr2[pk][:, :], func=AF.Exp, scale=C_M1),
                   tag=f"p3r2e{occ}_{i}")
        def _r2sig(e):
            e.activation(out=r2c[:, :, :], in_=r2c[:, :, :], func=AF.Ln, bias=C_ONE)
            return e.activation(out=r2c[:, :, :], in_=r2c[:, :, :], func=AF.Exp, scale=C_M1)
        b.step("act", ["r2c", "consts"], ["r2c"], _r2sig, tag=f"p3r2s{i}")
        b.step("sync", ["r2c"], ["r2d"],
               lambda e, t0=t0: e.dma_start(out=r2_d[:, :, t0:t0 + W3A], in_=r2c[:, :, :]),
               q="dsy", tag=f"p3sr2{i}")

    b.barrier()
    p3.close()

    W3B = 256
    N3B = NTOK // W3B
    p3b = contextlib.ExitStack()
    wk2_sb = p3b.enter_context(nc.sbuf_tensor("wk2_sb", [128, CC, I_FF], BF16))
    wv2_sb = p3b.enter_context(nc.sbuf_tensor("wv2_sb", [128, IC, C], BF16))
    b.step("pool", [], ["wk2"], lambda e: e.dma_start(out=wk2_sb[:, :, :], in_=wkey2.rearrange("(kc p) m -> p kc m", p=128)), q="dgp", tag="ldwk2")
    b.step("pool", [], ["wv2"], lambda e: e.dma_start(out=wv2_sb[:, :, :], in_=wval2.rearrange("(kc p) m -> p kc m", p=128)), q="dgp", tag="ldwv2")

    xkl = [p3b.enter_context(nc.sbuf_tensor(f"xkl{j}", [128, CC, W3B], BF16)) for j in range(2)]
    yl = [p3b.enter_context(nc.sbuf_tensor("yl0", [128, CC, W3B], F32))] * 2
    r2l = [p3b.enter_context(nc.sbuf_tensor("r2l0", [128, CC, W3B], F32))] * 2
    k2f = p3b.enter_context(nc.sbuf_tensor("k2f", [128, IC, W3B], BF16))
    kq = p3b.enter_context(nc.sbuf_tensor("kq", [128, 512], F32))
    x1c = p3b.enter_context(nc.sbuf_tensor("x1c", [128, CC, W3B], F32))
    tb3 = p3b.enter_context(nc.sbuf_tensor("tb3", [128, CC, W3B], F32))
    tb4 = p3b.enter_context(nc.sbuf_tensor("tb4", [128, CC, W3B], F32))
    xfb = p3b.enter_context(nc.sbuf_tensor("xfb", [128, CC, W3B], BF16))
    a4c = p3b.enter_context(nc.sbuf_tensor("a4c", [128, W3B], F32))
    ltst = p3b.enter_context(nc.sbuf_tensor("ltst", [8, W3B], F32))
    pk2 = [p3b.enter_context(nc.psum_tensor(f"pk2{j}", [128, 512], F32)) for j in range(2)]
    pcm = p3b.enter_context(nc.psum_tensor("pcm", [128, 1024], F32))
    pst2 = p3b.enter_context(nc.psum_tensor("pst2", [128, 512], F32))
    ptt = [p3b.enter_context(nc.psum_tensor(f"ptt{j}", [128, 512], F32)) for j in range(2)]

    for i in range(N3B):
        t0 = i * W3B
        p = i % 2
        b.step("sync", ["xk2d"], [f"xkl.{p}"],
               lambda e, t0=t0, p=p: e.dma_start(out=xkl[p][:, :, :], in_=xk2_d[:, :, t0:t0 + W3B]),
               q="dsy", tag=f"p3blx{i}")
        b.step("sync", ["yd"], ["yl"],
               lambda e, t0=t0, p=p: e.dma_start(out=yl[p][:, :, :], in_=y_d[:, :, t0:t0 + W3B]),
               q="dsy", tag=f"p3bly{i}")
        b.step("sync", ["r2d"], ["r2l"],
               lambda e, t0=t0, p=p: e.dma_start(out=r2l[p][:, :, :], in_=r2_d[:, :, t0:t0 + W3B]),
               q="dsy", tag=f"p3blr{i}")

        for g in range(IC // 2):
            def _k2mm(e, p=p, g=g):
                r = None
                for fl in range(2):
                    fc = 2 * g + fl
                    for kc in range(CC):
                        r = e.matmul(pk2[g % 2][:, fl * W3B:(fl + 1) * W3B],
                                     wk2_sb[:, kc, fc * 128:(fc + 1) * 128],
                                     xkl[p][:, kc, :],
                                     start=(kc == 0 and fl == 0), stop=(kc == CC - 1))
                    if fl == 0:
                        pass
                return r
            b.step("pe", [f"xkl.{p}", "wk2"], [f"pk2.{g % 2}"], _k2mm, tag=f"p3k2m{g}_{i}")
            b.step("dve", [f"pk2.{g % 2}"], ["kq"],
                   lambda e, g=g: e.tensor_copy(kq[:, :], pk2[g % 2][:, :]), tag=f"p3k2e{g}_{i}")
            b.step("dve", ["kq"], ["k2f"],
                   lambda e, g=g: e.scalar_tensor_tensor(
                       k2f[:, :, :].rearrange("q fc t -> q (fc t)")[:, 2 * g * W3B:(2 * g + 2) * W3B],
                       kq[:, :], 0.0, kq[:, :], OP.max, OP.mult),
                   tag=f"p3k2r{g}_{i}")
        for ch in range(2):
            def _cmm(e, ch=ch):
                r = None
                for fc in range(IC):
                    for ol in range(4):
                        occ = 4 * ch + ol
                        r = e.matmul(pcm[:, ol * W3B:(ol + 1) * W3B],
                                     wv2_sb[:, fc, occ * 128:(occ + 1) * 128],
                                     k2f[:, fc, :],
                                     start=(fc == 0 and ol % 2 == 0),
                                     stop=(fc == IC - 1))
                return r
            b.step("pe", ["k2f", "wv2"], ["pcm"], _cmm, tag=f"p3cm{ch}_{i}")
            def _x1h(e, ch=ch, p=p):
                e.tensor_tensor(tb3[:, 4 * ch:4 * ch + 4, :], r2l[p][:, 4 * ch:4 * ch + 4, :],
                                pcm[:, :].rearrange("q (ol t) -> q ol t", ol=4), OP.mult)
                return e.tensor_tensor(x1c[:, 4 * ch:4 * ch + 4, :], yl[p][:, 4 * ch:4 * ch + 4, :],
                                       tb3[:, 4 * ch:4 * ch + 4, :], OP.add)
            b.step("dve", ["pcm", "r2l", "yl"], ["x1c", "tb3"], _x1h, tag=f"p3x1{ch}_{i}")

        def _m4(e):
            r = None
            for kc in range(CC):
                r = e.matmul(pst2[:, 0:W3B], onesC, x1c[:, kc, :],
                             start=(kc == 0), stop=(kc == CC - 1))
            return r
        b.step("pe", ["x1c", "consts"], ["pst2"], _m4, tag=f"p3m4{i}")
        b.step("dve", ["x1c", "pst2"], ["tb3"],
               lambda e: e.tensor_tensor(tb3[:, :, :], x1c[:, :, :],
                                         bc(pst2[:, 0:W3B], CC), OP.subtract),
               tag=f"p3z4{i}")
        b.step("act", ["tb3"], ["tb4"],
               lambda e: e.activation(out=tb4[:, :, :], in_=tb3[:, :, :], func=AF.Square),
               tag=f"p3sq4{i}")
        def _v4(e):
            r = None
            for kc in range(CC):
                r = e.matmul(pst2[:, 256:256 + W3B], onesC, tb4[:, kc, :],
                             start=(kc == 0), stop=(kc == CC - 1))
            return r
        b.step("pe", ["tb4", "consts"], ["pst2"], _v4, tag=f"p3v4{i}")
        def _a4(e):
            e.activation(out=a4c[:, :], in_=pst2[:, 256:256 + W3B], func=AF.Ln, bias=C_E5)
            return e.activation(out=a4c[:, :], in_=a4c[:, :], func=AF.Exp, scale=C_MH)
        b.step("act", ["pst2", "consts"], ["a4c"], _a4, tag=f"p3a4{i}")
        b.step("dve", ["tb3", "a4c"], ["tb4"],
               lambda e: e.tensor_tensor(tb4[:, :, :], tb3[:, :, :], bc(a4c[:, :], CC), OP.mult),
               tag=f"p3xf{i}")
        b.step("dve", ["x1c", "tb4"], ["x1c"],
               lambda e: e.tensor_tensor(x1c[:, :, :], x1c[:, :, :], tb4[:, :, :], OP.add),
               tag=f"p3s{i}")
        b.step("dve", ["tb4"], ["xfb"],
               lambda e: e.tensor_copy(xfb[:, :, :], tb4[:, :, :]), tag=f"p3xfb{i}")

        def _lg(e):
            r = None
            for kc in range(CC):
                r = e.matmul(pst2[0:8, 256:256 + W3B], routersb[:, kc, :], xfb[:, kc, :],
                             start=(kc == 0), stop=(kc == CC - 1))
            return r
        b.step("pe", ["xfb", "routersb"], ["pst2"], _lg, tag=f"p3lg{i}")
        b.step("dve", ["pst2"], ["ltst"],
               lambda e: e.tensor_copy(ltst[:, :], pst2[0:8, 256:256 + W3B]), tag=f"p3lge{i}")
        for half in range(2):
            b.step("pe", ["ltst", "consts"], ["ptt.1"],
                   lambda e, half=half: e.transpose(ptt[1][:, half * 8:half * 8 + 8],
                                                    ltst[:, half * 128:(half + 1) * 128],
                                                    ident[0:8, 0:8]),
                   tag=f"p3ltt{half}_{i}")
        b.step("dve", ["ptt.1"], ["ltall"],
               lambda e, i=i: e.tensor_copy(
                   lt_all[:, 2 * i:2 * i + 2, :],
                   ptt[1][:, 0:16].rearrange("q (h e) -> q h e", h=2)),
               tag=f"p3lte{i}")

        for half in range(2):
            def _txf(e, half=half):
                r = None
                pv = ptt[0][:, :].bitcast(BF16)
                for cc in range(CC):
                    r = e.transpose(pv[:, cc * 128:(cc + 1) * 128],
                                    xfb[:, cc, half * 128:half * 128 + 128], identbf[:, :])
                return r
            b.step("pe", ["xfb", "identbf"], ["ptt.0"], _txf, tag=f"p3txf{half}_{i}")
            b.step("dve", ["ptt.0"], ["kq"],
                   lambda e: e.tensor_copy(kq[:, :].bitcast(BF16)[:, 0:C],
                                           ptt[0][:, :].bitcast(BF16)[:, 0:C]),
                   tag=f"p3txfe{half}_{i}")
            b.step("sync", ["kq"], ["xfd"],
                   lambda e, t0=t0, half=half: e.dma_start(
                       out=xf_d[t0 + half * 128:t0 + half * 128 + 128, :],
                       in_=kq[:, :].bitcast(BF16)[:, 0:C]),
                   q="dsy", tag=f"p3sxf{half}_{i}")
        for half in range(2):
            for sub in range(2):
                def _ts(e, half=half, sub=sub):
                    r = None
                    for j in range(4):
                        cc = sub * 4 + j
                        r = e.transpose(ptt[1][:, j * 128:(j + 1) * 128],
                                        x1c[:, cc, half * 128:half * 128 + 128], ident)
                    return r
                b.step("pe", ["x1c", "consts"], ["ptt.1"], _ts, tag=f"p3ts{half}_{sub}_{i}")
                b.step("dve", ["ptt.1"], ["tb3"],
                       lambda e, sub=sub: e.tensor_copy(
                           tb3[:, :, :].rearrange("q cc t -> q (cc t)")[:, sub * 512:(sub + 1) * 512],
                           ptt[1][:, :]),
                       tag=f"p3tse{half}_{sub}_{i}")
            b.step("sync", ["tb3"], ["sd"],
                   lambda e, t0=t0, half=half: e.dma_start(
                       out=s_d[t0 + half * 128:t0 + half * 128 + 128, :],
                       in_=tb3[:, :, :].rearrange("q cc t -> q (cc t)")[:, 0:C]),
                   q="dsy", tag=f"p3ss{half}_{i}")

    b.barrier()
    p3b.close()

    p4 = contextlib.ExitStack()
    m1c = p4.enter_context(nc.sbuf_tensor("m1c", [128, 1], F32))
    m2c = p4.enter_context(nc.sbuf_tensor("m2c", [128, 1], F32))
    dcol = p4.enter_context(nc.sbuf_tensor("dcol", [128, 1], F32))
    tcol = p4.enter_context(nc.sbuf_tensor("tcol", [128, 1], F32))
    tcol2 = p4.enter_context(nc.sbuf_tensor("tcol2", [128, 1], F32))
    eqc = p4.enter_context(nc.sbuf_tensor("eqc", [128, NEXP], F32))
    mic = p4.enter_context(nc.sbuf_tensor("mic", [128, NEXP], F32))
    ltc2 = p4.enter_context(nc.sbuf_tensor("ltc2", [128, NEXP], F32))
    o1c = p4.enter_context(nc.sbuf_tensor("o1c", [128, NEXP], F32))
    o2c = p4.enter_context(nc.sbuf_tensor("o2c", [128, NEXP], F32))
    occ_ = p4.enter_context(nc.sbuf_tensor("occ_", [128, NEXP], F32))
    scc = p4.enter_context(nc.sbuf_tensor("scc", [128, NEXP], F32))
    bsc = p4.enter_context(nc.sbuf_tensor("bsc", [128, NEXP], F32))
    scf = p4.enter_context(nc.sbuf_tensor("scf", [128, NEXP], F32))
    tm8 = p4.enter_context(nc.sbuf_tensor("tm8", [128, NEXP], F32))
    cntr = p4.enter_context(nc.sbuf_tensor("cntr", [1, NEXP], F32))
    ccsb = p4.enter_context(nc.sbuf_tensor("ccsb", [NEXP, NEXP], F32))
    offr = p4.enter_context(nc.sbuf_tensor("offr", [1, NEXP], F32))
    offb = p4.enter_context(nc.sbuf_tensor("offb", [128, NEXP], F32))
    xft = [p4.enter_context(nc.sbuf_tensor(f"xft{j}", [128, C], BF16)) for j in range(2)]
    pS4 = p4.enter_context(nc.psum_tensor("pS4", [128, 512], F32))
    pBase = p4.enter_context(nc.psum_tensor("pBase", [128, 512], F32))
    pOff4 = p4.enter_context(nc.psum_tensor("pOff4", [128, 512], F32))

    AX = mybir.AxisListType.X

    m1w = p4.enter_context(nc.sbuf_tensor("m1w", [128, N3T], F32))
    m2w = p4.enter_context(nc.sbuf_tensor("m2w", [128, N3T], F32))
    tm32a = p4.enter_context(nc.sbuf_tensor("tm32a", [128, N3T], F32))
    tm32b = p4.enter_context(nc.sbuf_tensor("tm32b", [128, N3T], F32))
    tm32c = p4.enter_context(nc.sbuf_tensor("tm32c", [128, N3T], F32))
    eqw = p4.enter_context(nc.sbuf_tensor("eqw", [128, N3T, NEXP], F32))
    miw = p4.enter_context(nc.sbuf_tensor("miw", [128, N3T, NEXP], F32))
    o1w = p4.enter_context(nc.sbuf_tensor("o1w", [128, N3T, NEXP], F32))
    o2w = p4.enter_context(nc.sbuf_tensor("o2w", [128, N3T, NEXP], F32))
    oww = p4.enter_context(nc.sbuf_tensor("oww", [128, N3T, NEXP], F32))
    ltc2w = p4.enter_context(nc.sbuf_tensor("ltc2w", [128, N3T, NEXP], F32))
    sccall = p4.enter_context(nc.sbuf_tensor("sccall", [128, N3T, NEXP], F32))
    bscall = p4.enter_context(nc.sbuf_tensor("bscall", [128, N3T, NEXP], F32))
    tsc4 = p4.enter_context(nc.sbuf_tensor("tsc4", [128, N3T, NEXP], F32))

    def bc3(ap2):
        return ap2.unsqueeze(2).broadcast_to([128, N3T, NEXP])

    iotaB = iota8.unsqueeze(1).broadcast_to([128, N3T, NEXP])

    _relay_state = {"tok": None, "n": 0}

    def dstep(reads, writes, emit, tag):
        r = list(reads) + ([_relay_state["tok"]] if _relay_state["tok"] else [])
        b.step("dve", r, writes, emit, tag=tag)
        tok = f"_p4tok{_relay_state['n']}"
        _relay_state["n"] += 1
        b.step("pool", list(writes), [tok], lambda e: e.nop(), tag=f"relay_{tag}")
        _relay_state["tok"] = tok

    dstep(["ltall"], ["m1w"],
           lambda e: e.tensor_reduce(m1w[:, :], lt_all[:, :, :], axis=AX, op=OP.max), tag="p4m1")
    dstep(["ltall", "m1w"], ["eqw"],
           lambda e: e.tensor_tensor(eqw[:, :, :], lt_all[:, :, :], bc3(m1w[:, :]), OP.is_equal), tag="p4eq1")
    dstep(["eqw"], ["tsc4"],
           lambda e: e.tensor_scalar(tsc4[:, :, :], eqw[:, :, :], -1000.0, None, OP.mult), tag="p4mi1a")
    dstep(["tsc4", "consts"], ["miw"],
           lambda e: e.tensor_tensor(miw[:, :, :], tsc4[:, :, :], iotaB, OP.add), tag="p4mi1")
    dstep(["miw"], ["tm32a"],
           lambda e: e.tensor_reduce(tm32a[:, :], miw[:, :, :], axis=AX, op=OP.min), tag="p4rm1")
    dstep(["tm32a"], ["e1a"],
           lambda e: e.tensor_scalar(e1a[:, :], tm32a[:, :], 1000.0, None, OP.add), tag="p4e1")
    dstep(["e1a", "consts"], ["o1w"],
           lambda e: e.tensor_tensor(o1w[:, :, :], iotaB, bc3(e1a[:, :]), OP.is_equal), tag="p4o1")
    dstep(["o1w"], ["tsc4"],
           lambda e: e.tensor_scalar(tsc4[:, :, :], o1w[:, :, :], -1.0e9, None, OP.mult), tag="p4lt2a")
    dstep(["tsc4", "ltall"], ["ltc2w"],
           lambda e: e.tensor_tensor(ltc2w[:, :, :], tsc4[:, :, :], lt_all[:, :, :], OP.add), tag="p4lt2")
    dstep(["ltc2w"], ["m2w"],
           lambda e: e.tensor_reduce(m2w[:, :], ltc2w[:, :, :], axis=AX, op=OP.max), tag="p4m2")
    dstep(["ltc2w", "m2w"], ["eqw"],
           lambda e: e.tensor_tensor(eqw[:, :, :], ltc2w[:, :, :], bc3(m2w[:, :]), OP.is_equal), tag="p4eq2")
    dstep(["eqw"], ["tsc4"],
           lambda e: e.tensor_scalar(tsc4[:, :, :], eqw[:, :, :], -1000.0, None, OP.mult), tag="p4mi2a")
    dstep(["tsc4", "consts"], ["miw"],
           lambda e: e.tensor_tensor(miw[:, :, :], tsc4[:, :, :], iotaB, OP.add), tag="p4mi2")
    dstep(["miw"], ["tm32a"],
           lambda e: e.tensor_reduce(tm32a[:, :], miw[:, :, :], axis=AX, op=OP.min), tag="p4rm2")
    b.step("pool", ["tm32a"], ["e2a"],
           lambda e: e.tensor_scalar(e2a[:, :], tm32a[:, :], 1000.0, None, OP.add), tag="p4e2")
    dstep(["m1w", "m2w"], ["tm32a"],
           lambda e: e.tensor_tensor(tm32a[:, :], m1w[:, :], m2w[:, :], OP.subtract), tag="p4dm")
    def _g1(e):
        e.activation(out=g1a[:, :], in_=tm32a[:, :], func=AF.Exp, scale=C_M1)
        e.activation(out=g1a[:, :], in_=g1a[:, :], func=AF.Ln, bias=C_ONE)
        return e.activation(out=g1a[:, :], in_=g1a[:, :], func=AF.Exp, scale=C_M1)
    b.step("act", ["tm32a", "consts"], ["g1a"], _g1, tag="p4g1")
    dstep(["g1a"], ["g2a"],
           lambda e: e.tensor_scalar(g2a[:, :], g1a[:, :], -1.0, 1.0, OP.mult, OP.add), tag="p4g2")
    dstep(["e2a", "consts"], ["o2w"],
           lambda e: e.tensor_tensor(o2w[:, :, :], iotaB, bc3(e2a[:, :]), OP.is_equal), tag="p4o2")
    b.step("pool", ["o1w", "o2w"], ["oww"],
           lambda e: e.tensor_tensor(oww[:, :, :], o1w[:, :, :], o2w[:, :, :], OP.add), tag="p4ow")
    b.step("pool", [], ["bscall"],
           lambda e: e.memset(bscall[:, 0, :], 0.0), tag="p4bz")

    for i in range(N3T):
        b.step("pe", ["oww", "consts"], ["pS4"],
               lambda e, i=i: e.matmul(pS4[:, 0:NEXP], LT128, oww[:, i, :], start=True, stop=True),
               tag=f"p4sc{i}")
        b.step("act", ["pS4"], ["sccall"],
               lambda e, i=i: e.copy(sccall[:, i, :], pS4[:, 0:NEXP]), tag=f"p4sce{i}")
        if i > 0:
            b.step("act", ["pBase"], ["bscall"],
                   lambda e, i=i: e.copy(bscall[:, i, :], pBase[:, 0:NEXP]), tag=f"p4be{i}")
        b.step("pe", ["oww", "consts", "bscall"], ["pBase"],
               lambda e, i=i: e.matmul(pBase[:, 0:NEXP], ones128, oww[:, i, :],
                                       start=(i == 0), stop=(i == N3T - 1)),
               tag=f"p4ba{i}")

    dstep(["sccall", "bscall"], ["eqw"],
           lambda e: e.tensor_tensor(eqw[:, :, :], sccall[:, :, :], bscall[:, :, :], OP.add), tag="p4scf")
    b.step("pool", ["eqw", "o1w"], ["miw"],
           lambda e: e.tensor_tensor(miw[:, :, :], eqw[:, :, :], o1w[:, :, :], OP.mult), tag="p4rs1")
    dstep(["miw"], ["r1a"],
           lambda e: e.tensor_reduce(r1a[:, :], miw[:, :, :], axis=AX, op=OP.add), tag="p4r1")
    b.step("pool", ["eqw", "o2w"], ["miw"],
           lambda e: e.tensor_tensor(miw[:, :, :], eqw[:, :, :], o2w[:, :, :], OP.mult), tag="p4rs2")
    dstep(["miw"], ["r2a"],
           lambda e: e.tensor_reduce(r2a[:, :], miw[:, :, :], axis=AX, op=OP.add), tag="p4r2")

    b.step("act", ["pBase"], ["cntr"],
           lambda e: e.copy(cntr[:, :], pBase[0:1, 0:NEXP]), tag="p4cnt")
    b.step("sync", ["cntr"], ["ccin"],
           lambda e: e.dma_start(out=ccin_d[:, :], in_=cntr[:, :]), q="dsy", tag="p4ccs")
    if USE_COLLECTIVE:
        b.step("pool", ["ccin"], ["ccout"],
               lambda e: e.collective_compute(
                   "AllGather", OP.bypass, replica_groups=[list(range(NCORES))],
                   ins=[ccin_d[:, :]], outs=[ccout_d[:, :]]),
               tag="p4cc")
        b.step("pool", ["ccout"], ["ccsb"],
               lambda e: e.dma_start(out=ccsb[:, :], in_=ccout_d[:, :]), q="dgp", tag="p4ccl")
    else:
        dstep(["ccin"], ["ccsb"],
               lambda e: e.memset(ccsb[:, :], 0.0), tag="p4cc0")
    b.step("pe", ["ccsb", "ltsb"], ["pOff4"],
           lambda e: e.matmul(pOff4[0:1, 0:NEXP], ltsb[:, :], ccsb[:, :], start=True, stop=True),
           tag="p4off")
    b.step("act", ["pOff4"], ["offr"],
           lambda e: e.copy(offr[:, :], pOff4[0:1, 0:NEXP]), tag="p4offe")
    b.step("pe", ["offr", "consts"], ["pOff4"],
           lambda e: e.matmul(pOff4[:, 8:8 + NEXP], ones128[0:1, :], offr[:, :],
                              start=True, stop=True),
           tag="p4offb")
    dstep(["pOff4"], ["offb"],
           lambda e: e.tensor_copy(offb[:, :], pOff4[:, 8:8 + NEXP]), tag="p4offc")

    offbB = offb[:, :].unsqueeze(1).broadcast_to([128, N3T, NEXP])
    for side, ow_s, ra, ga, gts, sli, ea in (
            (1, o1w, r1a, g1a, gts1, sl1i, e1a), (2, o2w, r2a, g2a, gts2, sl2i, e2a)):
        dstep(["offb", f"o{side}w"], ["miw"],
               lambda e, ow_s=ow_s: e.tensor_tensor(miw[:, :, :], offbB, ow_s[:, :, :], OP.mult),
               tag=f"p4of{side}")
        dstep(["miw"], ["tm32a"],
               lambda e: e.tensor_reduce(tm32a[:, :], miw[:, :, :], axis=AX, op=OP.add),
               tag=f"p4ofr{side}")
        b.step("pool", ["tm32a", f"r{side}a"], ["tm32b"],
               lambda e, ra=ra: e.tensor_tensor(tm32b[:, :], tm32a[:, :], ra[:, :], OP.add),
               tag=f"p4gl{side}")
        dstep(["tm32b"], ["tm32a"],
               lambda e: e.tensor_scalar(tm32a[:, :], tm32b[:, :], float(CAPACITY), None, OP.is_lt),
               tag=f"p4k{side}")
        b.step("pool", [f"r{side}a"], ["tm32c"],
               lambda e, ra=ra: e.tensor_scalar(tm32c[:, :], ra[:, :], float(SLOTCAP), None, OP.is_lt),
               tag=f"p4kl{side}")
        dstep(["tm32a", "tm32c"], ["tm32b"],
               lambda e: e.tensor_tensor(tm32b[:, :], tm32a[:, :], tm32c[:, :], OP.mult),
               tag=f"p4kp{side}")
        b.step("pool", ["tm32b", f"g{side}a"], [f"gts{side}"],
               lambda e, ga=ga, gts=gts: e.tensor_tensor(gts[:, :], tm32b[:, :], ga[:, :], OP.mult),
               tag=f"p4gt{side}")
        dstep([f"r{side}a"], ["tm32a"],
               lambda e, ra=ra: e.tensor_scalar(tm32a[:, :], ra[:, :], float(SLOTCAP - 1), None, OP.min),
               tag=f"p4sm{side}")
        dstep([f"e{side}a", "tm32a"], ["tm32b"],
               lambda e, ea=ea: e.scalar_tensor_tensor(tm32b[:, :], ea[:, :], float(SLOTCAP),
                                                       tm32a[:, :], OP.mult, OP.add),
               tag=f"p4sf{side}")
        dstep(["tm32b"], ["tm32b"],
               lambda e: e.tensor_scalar(tm32b[:, :], tm32b[:, :], 0.0, float(NSLOT - 1),
                                         OP.max, OP.min),
               tag=f"p4sc{side}")
        dstep(["tm32b"], [f"sl{side}i"],
               lambda e, sli=sli: e.tensor_copy(sli[:, :], tm32b[:, :]), tag=f"p4si{side}")

    if "dbg" in DEBUG_TAPS:
        b.step("sync", ["ltall"], ["dbglt"], lambda e: e.dma_start(out=dbg_lt[:, :, :], in_=lt_all[:, :, :]), q="dsy", tag="dbglt")
        b.step("sync", ["e1a"], ["dbge1"], lambda e: e.dma_start(out=dbg_e1[:, :], in_=e1a[:, :]), q="dsy", tag="dbge1")
        b.step("sync", ["r1a"], ["dbgr1"], lambda e: e.dma_start(out=dbg_r1[:, :], in_=r1a[:, :]), q="dsy", tag="dbgr1")
        b.step("sync", ["sl1i"], ["dbgsl"], lambda e: e.dma_start(out=dbg_sl1[:, :], in_=sl1i[:, :]), q="dsy", tag="dbgsl1")
        b.step("sync", ["sl2i"], ["dbgsl"], lambda e: e.dma_start(out=dbg_sl2[:, :], in_=sl2i[:, :]), q="dsy", tag="dbgsl2")
        b.step("sync", ["gts1"], ["dbgg"], lambda e: e.dma_start(out=dbg_g1[:, :], in_=gts1[:, :]), q="dsy", tag="dbgg1")
        b.step("sync", ["gts2"], ["dbgg"], lambda e: e.dma_start(out=dbg_g2[:, :], in_=gts2[:, :]), q="dsy", tag="dbgg2")

    for i in range(N3T):
        t0 = i * W3T
        p = i % 2
        b.step("sync", ["xfd"], [f"xft.{p}"],
               lambda e, t0=t0, p=p: e.dma_start(out=xft[p][:, :], in_=xf_d[t0:t0 + W3T, :]),
               q="dsy", tag=f"p4xl{i}")
        b.step("pool", [f"xft.{p}", "sl1i"], ["bufs"],
               lambda e, i=i, p=p: e.indirect_dma_start(
                   out=buf_d[:, :], out_offset=IndirectOffsetOnAxis(ap=sl1i[:, i:i + 1], axis=0),
                   in_=xft[p][:, :], in_offset=None),
               q="dgp", tag=f"p4s1{i}")
        b.step("pool", [f"xft.{p}", "sl2i"], ["bufs"],
               lambda e, i=i, p=p: e.indirect_dma_start(
                   out=buf_d[:, :], out_offset=IndirectOffsetOnAxis(ap=sl2i[:, i:i + 1], axis=0),
                   in_=xft[p][:, :], in_offset=None),
               q="dgp", tag=f"p4s2{i}")

    b.barrier()
    p4.close()

    p5 = contextlib.ExitStack()
    SLOTGRP = 512
    NGRP = SLOTCAP // SLOTGRP
    w2e_sb = p5.enter_context(nc.sbuf_tensor("w2e_sb", [128, IC, C], BF16))
    w1sb = [p5.enter_context(nc.sbuf_tensor(f"w1sb{j}", [128, CC, 512], BF16)) for j in range(2)]
    gtb = [p5.enter_context(nc.sbuf_tensor(f"gtb{j}", [128, 4, C], BF16)) for j in range(2)]
    xfm = [p5.enter_context(nc.sbuf_tensor(f"xfm{j}", [128, CC, SLOTGRP], BF16)) for j in range(2)]
    h_sb = p5.enter_context(nc.sbuf_tensor("h_sb", [128, IC, SLOTGRP], BF16))
    om = p5.enter_context(nc.sbuf_tensor("om", [128, CC, SLOTGRP], F32))
    postg = [p5.enter_context(nc.sbuf_tensor(f"postg{j}", [128, C], F32)) for j in range(2)]
    ptin_ = [p5.enter_context(nc.psum_tensor(f"ptin{j}", [128, 512], F32)) for j in range(2)]
    pm1 = [p5.enter_context(nc.psum_tensor(f"pm1{j}", [128, 512], F32)) for j in range(2)]
    pm2 = [p5.enter_context(nc.psum_tensor(f"pm2{j}", [128, 512], F32)) for j in range(2)]
    pto = p5.enter_context(nc.psum_tensor("pto", [128, 1024], F32))

    for ex in range(NEXP):
        for fg in range(CC):
            def _w2(e, ex=ex, fg=fg):
                return e.dma_start(
                    out=w2e_sb[:, fg * 4:(fg + 1) * 4, :],
                    in_=w2[ex, fg * 512:(fg + 1) * 512, :].rearrange("(fc p) c -> p fc c", p=128))
            b.step("pool", ["w2e"], ["w2e"], _w2, q="dgp", tag=f"p5w2{ex}_{fg}")
        for g in range(NGRP):
            sbase = ex * SLOTCAP + g * SLOTGRP
            gp = g % 2
            b.step("act", ["bufs"], [f"gtb.{gp}"],
                   lambda e, sbase=sbase, gp=gp: e.dma_start(
                       out=gtb[gp][:, :, :],
                       in_=buf_d[sbase:sbase + SLOTGRP, :].rearrange("(a b) c -> b a c", b=128)),
                   q="dsc", tag=f"p5g{ex}_{g}")
            for j in range(SLOTGRP // 128):
                jp = j % 2
                def _tin(e, gp=gp, j=j, jp=jp):
                    r = None
                    pv = ptin_[jp][:, :].bitcast(BF16)
                    for cc in range(CC):
                        r = e.transpose(pv[:, cc * 128:(cc + 1) * 128],
                                        gtb[gp][:, j, cc * 128:(cc + 1) * 128], identbf[:, :])
                    return r
                b.step("pe", [f"gtb.{gp}", "identbf"], [f"ptin.{jp}"], _tin, tag=f"p5t{ex}_{g}_{j}")
                b.step("dve", [f"ptin.{jp}"], [f"xfm.{gp}"],
                       lambda e, gp=gp, j=j, jp=jp: e.tensor_copy(
                           xfm[gp][:, :, j * 128:(j + 1) * 128],
                           ptin_[jp][:, :].bitcast(BF16)[:, 0:C].rearrange("p (cc t) -> p cc t", cc=CC)),
                       tag=f"p5te{ex}_{g}_{j}")
            for fg in range(CC):
                wp = fg % 2
                def _w1(e, ex=ex, fg=fg, wp=wp):
                    return e.dma_start(
                        out=w1sb[wp][:, :, :],
                        in_=w1[ex, :, fg * 512:(fg + 1) * 512].rearrange("(kc p) f -> p kc f", p=128))
                b.step("pool", [f"w1b.{wp}"], [f"w1b.{wp}"], _w1, q="dgp", tag=f"p5w1{ex}_{g}_{fg}")
                for fl in range(4):
                    fc = fg * 4 + fl
                    def _m1(e, fl=fl, wp=wp, fc=fc, gp=gp):
                        r = None
                        for kc in range(CC):
                            r = e.matmul(pm1[fc % 2][:, :],
                                         w1sb[wp][:, kc, fl * 128:(fl + 1) * 128],
                                         xfm[gp][:, kc, :],
                                         start=(kc == 0), stop=(kc == CC - 1))
                        return r
                    b.step("pe", [f"w1b.{wp}", f"xfm.{gp}"], [f"pm1.{fc % 2}"], _m1,
                           tag=f"p5m1{ex}_{g}_{fc}")
                    b.step("act", [f"pm1.{fc % 2}"], ["h_sb"],
                           lambda e, fc=fc: e.activation(out=h_sb[:, fc, :], in_=pm1[fc % 2][:, :],
                                                         func=AF.Silu),
                           tag=f"p5si{ex}_{g}_{fc}")
            for occ in range(CC):
                pp2 = occ % 2
                def _m2(e, occ=occ, pp2=pp2):
                    r = None
                    for fc in range(IC):
                        r = e.matmul(pm2[pp2][:, :],
                                     w2e_sb[:, fc, occ * 128:(occ + 1) * 128],
                                     h_sb[:, fc, :],
                                     start=(fc == 0), stop=(fc == IC - 1))
                    return r
                b.step("pe", ["w2e", "h_sb"], [f"pm2.{pp2}"], _m2, tag=f"p5m2{ex}_{g}_{occ}")
                b.step("dve", [f"pm2.{pp2}"], ["om"],
                       lambda e, occ=occ, pp2=pp2: e.tensor_copy(om[:, occ, :], pm2[pp2][:, :]),
                       tag=f"p5oe{ex}_{g}_{occ}")
            for j in range(SLOTGRP // 128):
                jp = j % 2
                def _tout(e, j=j):
                    r = None
                    for cc in range(CC):
                        r = e.transpose(pto[:, cc * 128:(cc + 1) * 128],
                                        om[:, cc, j * 128:(j + 1) * 128], ident)
                    return r
                b.step("pe", ["om", "consts"], ["pto"], _tout, tag=f"p5to{ex}_{g}_{j}")
                b.step("dve", ["pto"], [f"postg.{jp}"],
                       lambda e, jp=jp: e.tensor_copy(postg[jp][:, :], pto[:, 0:C]),
                       tag=f"p5toe{ex}_{g}_{j}")
                b.step("sync", [f"postg.{jp}"], ["pod"],
                       lambda e, sbase=sbase, j=j, jp=jp: e.dma_start(
                           out=po_d[sbase + j * 128:sbase + (j + 1) * 128, :], in_=postg[jp][:, :]),
                       q="dsy", tag=f"p5ps{ex}_{g}_{j}")

        b.barrier()
    p5.close()

    p6 = contextlib.ExitStack()
    gb1 = [p6.enter_context(nc.sbuf_tensor(f"gb1{j}", [128, C], F32)) for j in range(2)]
    gb2 = [p6.enter_context(nc.sbuf_tensor(f"gb2{j}", [128, C], F32)) for j in range(2)]
    sld = [p6.enter_context(nc.sbuf_tensor(f"sld{j}", [128, C], F32)) for j in range(2)]
    ta6 = [p6.enter_context(nc.sbuf_tensor(f"ta6{j}", [128, C], F32)) for j in range(2)]
    ot6 = [p6.enter_context(nc.sbuf_tensor(f"ot6{j}", [128, C], F32)) for j in range(2)]

    for i in range(N3T):
        t0 = i * W3T
        p = i % 2
        b.step("pool", ["pod", "sl1i"], [f"gb1.{p}"],
               lambda e, i=i, p=p: e.indirect_dma_start(
                   out=gb1[p][:, :], out_offset=None, in_=po_d[:, :],
                   in_offset=IndirectOffsetOnAxis(ap=sl1i[:, i:i + 1], axis=0)),
               q="dgp", tag=f"p6g1{i}")
        b.step("pool", ["pod", "sl2i"], [f"gb2.{p}"],
               lambda e, i=i, p=p: e.indirect_dma_start(
                   out=gb2[p][:, :], out_offset=None, in_=po_d[:, :],
                   in_offset=IndirectOffsetOnAxis(ap=sl2i[:, i:i + 1], axis=0)),
               q="dgp", tag=f"p6g2{i}")
        b.step("sync", ["sd"], [f"sld.{p}"],
               lambda e, t0=t0, p=p: e.dma_start(out=sld[p][:, :], in_=s_d[t0:t0 + W3T, :]),
               q="dsy", tag=f"p6ls{i}")
        b.step("act", [f"gb1.{p}", "gts1"], [f"ta6.{p}"],
               lambda e, i=i, p=p: e.activation(out=ta6[p][:, :], in_=gb1[p][:, :],
                                                func=AF.Copy, scale=gts1[:, i:i + 1]),
               tag=f"p6a1{i}")
        b.step("dve", [f"ta6.{p}", f"sld.{p}"], [f"ot6.{p}"],
               lambda e, p=p: e.tensor_tensor(ot6[p][:, :], sld[p][:, :], ta6[p][:, :], OP.add),
               tag=f"p6x1{i}")
        b.step("act", [f"gb2.{p}", "gts2"], [f"ta6.{p}"],
               lambda e, i=i, p=p: e.activation(out=ta6[p][:, :], in_=gb2[p][:, :],
                                                func=AF.Copy, scale=gts2[:, i:i + 1]),
               tag=f"p6a2{i}")
        b.step("dve", [f"ta6.{p}", f"ot6.{p}"], [f"sld.{p}"],
               lambda e, p=p: e.tensor_tensor(sld[p][:, :], ot6[p][:, :], ta6[p][:, :], OP.add),
               tag=f"p6x2{i}")
        b.step("sync", [f"sld.{p}"], ["outd"],
               lambda e, t0=t0, p=p: e.dma_start(out=out_d[t0:t0 + W3T, :], in_=sld[p][:, :]),
               q="dsy", tag=f"p6o{i}")

    b.barrier()
    return nc, b


def build_nc():
    nc, b = build_program()
    with contextlib.ExitStack() as st:
        sems = {e: st.enter_context(nc.semaphore(f"sem_{e}")) for e in ENGN}
        dma_sems = {q: st.enter_context(nc.semaphore(f"dma_{q}"))
                    for q in ("dsy", "dsc", "dgp")}
        bar = st.enter_context(nc.semaphore("barr"))
        block = st.enter_context(nc.Block())
        b.lower(block, None, sems, dma_sems, bar)
    b.ctx.close()
    return nc



_NC_CACHE = []


def _host_consts():
    import ml_dtypes
    cst = np.zeros((128, 528), np.float32)
    cst[:, 0:128] = np.eye(128, dtype=np.float32)
    cst[:, 128:256] = np.triu(np.ones((128, 128), np.float32), 1)
    cst[:, 256:264] = np.tile(np.arange(8, dtype=np.float32), (128, 1))
    vals = [1.0, -1.0, -0.5, 1e-11, 1e-5, 1e-6, 1e-8, 1.0 + 1e-6]
    cst[:, 264:272] = np.asarray(vals, np.float32)[None, :]
    cst[:, 272:400] = 1.0 / 1024.0
    cst[:, 400:528] = 1.0
    identbf = np.eye(128).astype(ml_dtypes.bfloat16)
    return cst, identbf


def kernel(**inputs):
    f32 = lambda k: np.ascontiguousarray(np.asarray(inputs[k], dtype=np.float32))
    x = f32("L0_attn_x")
    wkey, wval, wrec = f32("L0_attn_W_key"), f32("L0_attn_W_value"), f32("L0_attn_W_recept")
    wout_, wrec2 = f32("L0_attn_W_out"), f32("L0_attn_W_recept2")
    wkey2, wval2 = f32("L0_attn_W_key2"), f32("L0_attn_W_value2")
    rw_ = f32("L0_ffn_router_w")
    w1_, w2_ = f32("L0_ffn_w1"), f32("L0_ffn_w2")

    def col(k):
        v = f32(k)
        return v.reshape(CC, 128).T
    vecs = np.stack([
        col("L0_attn_time_decay"), col("L0_attn_time_first"),
        col("L0_attn_time_mix_k"), col("L0_attn_time_mix_v"), col("L0_attn_time_mix_r"),
        col("L0_attn_time_mix_k2"), col("L0_attn_time_mix_r2"),
        col("L0_attn_ln1_w"), col("L0_attn_ln2_w"),
    ], axis=-1)
    vecs = np.ascontiguousarray(vecs)
    cst, identbf = _host_consts()
    rwt = np.ascontiguousarray(rw_.T)

    if not _NC_CACHE:
        _NC_CACHE.append(build_nc())
    nc = _NC_CACHE[0]

    in_maps = []
    for c in range(NCORES):
        xc = np.ascontiguousarray(
            x[c * NB:(c + 1) * NB].reshape(NTOK, C).T)
        ltc = (np.arange(NEXP) < c).astype(np.float32).reshape(NEXP, 1)
        in_maps.append({
            "x0f": xc, "wkey": wkey, "wval": wval, "wrec": wrec,
            "wout": wout_, "wrec2": wrec2, "wkey2": wkey2, "wval2": wval2,
            "rwt": rwt, "w1": w1_, "w2": w2_, "vecs_i": vecs,
            "consts_i": cst, "identbf_i": identbf, "ltcore": ltc,
        })
    res = run_bass_kernel_spmd(nc, in_maps, core_ids=list(range(NCORES)))
    out = np.empty((B_FULL, T, C), np.float32)
    for c, r in enumerate(res.results):
        out[c * NB:(c + 1) * NB] = np.asarray(r["outp"], np.float32).reshape(NB, T, C)
    return out
